# version 07
# baseline (speedup 1.0000x reference)
import numpy as np

import concourse.bass as bass
import concourse.mybir as mybir
from concourse.bass_utils import run_bass_kernel_spmd

F32 = mybir.dt.float32
BF16 = mybir.dt.bfloat16
AF = mybir.ActivationFunctionType
ALU = mybir.AluOpType
AXX = mybir.AxisListType.X

T, D, HID, DI, NS = 8192, 1024, 2816, 2048, 16
KD = D // 128
KH = DI // 128
NQ = T // 128
TBF = 512
NTBF = T // TBF
TSC = 256
NTS = T // TSC
TB3 = 512
NTB3 = T // TB3
NHT = HID // 128
CHUNKS = [(0, 6), (6, 12), (12, 18), (18, 22)]
CHMAX = 6
EPS = 1e-5
ENGINES = ("sync", "gpsimd", "vector", "scalar", "tensor")


def _noop(*a, **k):
    return None


class Shim:
    __slots__ = ("r",)

    def __init__(self, real):
        self.r = real

    def __getattr__(self, n):
        if self.r is None:
            return _noop
        return getattr(self.r, n)


class CSem:
    def __init__(self, h):
        self.h = h
        self.v = 0

    def inc(self, ins, n=1):
        self.v += n
        if ins is not None:
            ins.then_inc(self.h, n)
        return self.v


def build_nc(stop=None):
    nc = bass.Bass("TRN2", target_bir_lowering=False)

    x0 = nc.dram_tensor("x0", [T, D], F32, kind="ExternalInput").ap()
    wi = {}
    for name, shape in [
        ("norm_w", [D]), ("norm_b", [D]), ("W_in", [D, DI]), ("W_gate", [D, DI]),
        ("W_dt", [DI, DI]), ("b_dt", [DI]), ("W_B", [DI, NS]), ("W_C", [DI, NS]),
        ("A_log", [DI, NS]), ("W_out", [DI, D]),
        ("sh_gate", [D, HID]), ("sh_up", [D, HID]), ("sh_down", [HID, D]),
        ("r_gate", [7, D, HID]), ("r_up", [7, D, HID]), ("r_down", [7, HID, D]),
        ("Wr", [D, 7]), ("r_bias", [7]),
    ]:
        wi[name] = nc.dram_tensor(name, shape, F32, kind="ExternalInput").ap()
    out = nc.dram_tensor("out", [T, D], F32, kind="ExternalOutput").ap()

    x1 = nc.dram_tensor("x1", [T, D], F32, kind="Internal").ap()
    x2 = nc.dram_tensor("x2", [T, D], F32, kind="Internal").ap()
    xnbT = nc.dram_tensor("xnbT", [D, T], BF16, kind="Internal").ap()
    zth = nc.dram_tensor("zth", [DI, T], BF16, kind="Internal").ap()
    dth = nc.dram_tensor("dth", [DI, T], BF16, kind="Internal").ap()
    dtzh = nc.dram_tensor("dtzh", [DI, T], BF16, kind="Internal").ap()
    gateh = nc.dram_tensor("gateh", [DI, T], BF16, kind="Internal").ap()
    bch = nc.dram_tensor("bch", [NS, 2, T], BF16, kind="Internal").ap()

    from contextlib import ExitStack
    ctx = ExitStack()

    def sb(name, shape, dtype):
        return ctx.enter_context(nc.sbuf_tensor(name, shape, dtype))

    R = dict(x0=x0, wi=wi, out=out, x1=x1, x2=x2, xnbT=xnbT, zth=zth, dth=dth,
             dtzh=dtzh, gateh=gateh, bch=bch)
    R["POOLW"] = sb("POOLW", [128, 38912], BF16)
    R["XNT"] = sb("XNT", [128, 2 * KD * TBF], BF16)
    R["HT"] = sb("HT", [128, CHMAX * TBF], BF16)
    R["HSIL"] = sb("HSIL", [128, 2 * TBF], BF16)
    R["EV"] = sb("EV", [128, 2 * D], F32)
    R["STG"] = sb("STG", [128, 2 * D], BF16)
    R["WSB"] = sb("WSB", [128, NQ * 8], F32)
    R["WRSB"] = sb("WRSB", [128, KD * 8], F32)
    R["BIASR"] = sb("BIASR", [128, 8], F32)
    R["IDF"] = sb("IDF", [128, 128], F32)
    R["IDB"] = sb("IDB", [128, 128], BF16)
    R["XTI"] = sb("XTI", [128, 2 * D], F32)
    R["XN32"] = sb("XN32", [128, D], F32)
    R["XNT32"] = sb("XNT32", [128, D], F32)
    R["DUM"] = sb("DUM", [128, D], F32)
    R["STT"] = sb("STT", [128, 24], F32)
    R["EPSC"] = sb("EPSC", [128, 1], F32)
    R["POOLM"] = sb("POOLM", [128, 28672], BF16)
    R["ASC"] = sb("ASC", [128, KH * NS], F32)
    R["BDT"] = sb("BDT", [128, KH], F32)
    R["WBC"] = sb("WBC", [128, 2 * KH * NS], BF16)
    R["HST"] = sb("HST", [128, KH * NS], F32)
    R["pa"] = [ctx.enter_context(nc.psum_tensor(f"pa{i}", [128, 1024], F32))
               for i in range(4)]

    sems = {}
    for nm in ["t", "v", "s", "g", "d_w0", "d_w1", "d_a0", "d_a1",
               "d_b0", "d_b1", "d_st0", "d_st1", "d_acc", "d_bc0", "d_bc1",
               "d_cp"]:
        sems[nm] = ctx.enter_context(nc.semaphore(nm))

    b = Builder(nc, R, sems, stop)
    with nc.Block() as block:
        block.sync(lambda e: b.emit("sync", e))
        block.gpsimd(lambda e: b.emit("gpsimd", e))
        block.vector(lambda e: b.emit("vector", e))
        block.scalar(lambda e: b.emit("scalar", e))
        block.tensor(lambda e: b.emit("tensor", e))
    ctx.close()
    return nc


class Builder:
    def __init__(self, nc, R, sems, stop):
        self.nc = nc
        self.R = R
        self.sems = sems
        self.stop = stop

    def emit(self, which, eng):
        self.which = which
        E = {nm: Shim(eng if nm == which else None) for nm in ENGINES}
        self.t, self.v, self.s = E["tensor"], E["vector"], E["scalar"]
        self.sy, self.g = E["sync"], E["gpsimd"]
        self.E = E
        self.C = {nm: CSem(h) for nm, h in self.sems.items()}
        self.h_silu = [0, 0]; self.h_hmul = [0, 0]
        self.h_ev = [0, 0]; self.h_acc = [0, 0]
        self.h_xnt = [0, 0]; self.h_wchunk = [0, 0]
        self.h_p1ev = 0; self.h_p1v = 0; self.h_stg = [0, 0]
        self.h_x2st = [0, 0]; self.h_trev = 0; self.h_zev = [0, 0]
        self.h_gst = [0, 0]; self.h_zst = 0; self.h_p3v = 0
        self.h_dtv = [0, 0]; self.h_dtst = [0, 0]; self.h_bcev = 0
        self.h_scan = [0, 0, 0, 0]; self.h_dA = [0, 0, 0, 0]; self.h_ygst = 0
        self.h_sdone = [0, 0]; self.h_p4ld = 0; self.h_p4bmm = [0, 0]
        self.h_routerv = 0; self.h_xti = [0, 0]; self.h_xtis = [0, 0]
        self.h_trt = 0; self._wtarget = {}
        self.h_dtuse_s = [0, 0]; self.h_dtuse_v = [0, 0]
        self.h_p4bv = [0, 0]

        R = self.R
        self.prologue()
        self.barrier()
        self.p_ln_router(src=R["x0"], q_lo=0, q_hi=NQ)
        self.barrier()
        dst1 = R["out"] if self.stop == "ffn1" else R["x1"]
        self.copy_resid(src=R["x0"], dst=dst1)
        self.barrier()
        if self.stop == "ffn1":
            self.drain(self.p_experts(dst=dst1, q_lo=0, q_hi=NQ))
            self.barrier()
            self.epilogue()
            return
        if self.stop == "mamba":
            self.drain(self.p_experts(dst=dst1, q_lo=0, q_hi=NQ))
            self.barrier()
            self.p3a(src=dst1, tb_lo=0, tb_hi=NTB3)
            self.barrier()
            self.p3b(tb_lo=0, tb_hi=NTB3)
            self.barrier()
            self.drain(self.p4a(ts_lo=0, ts_hi=NTS))
            self.barrier()
            self.copy_resid(src=R["x2"], dst=R["out"])
            self.barrier()
            self.p4b(dst=R["out"], ts_lo=0, ts_hi=NTS)
            self.barrier()
            self.epilogue()
            return
        HQ, HTB, HTS = NQ // 2, NTB3 // 2, NTS // 2
        self.drain(self.p_experts(dst=dst1, q_lo=0, q_hi=HQ))
        self.barrier()
        self.p3a(src=dst1, tb_lo=0, tb_hi=HTB)
        self.barrier()
        self.p3b(tb_lo=0, tb_hi=HTB)
        self.barrier()
        self.interleave(self.p_experts(dst=dst1, q_lo=HQ, q_hi=NQ),
                        self.p4a(ts_lo=0, ts_hi=HTS))
        self.barrier()
        self.p3a(src=dst1, tb_lo=HTB, tb_hi=NTB3)
        self.barrier()
        self.p3b(tb_lo=HTB, tb_hi=NTB3)
        self.barrier()
        self.p4b(dst=R["x2"], ts_lo=0, ts_hi=HTS)
        self.barrier()
        self.p_ln_router(src=R["x2"], q_lo=0, q_hi=HQ)
        self.barrier()
        self.copy_resid(src=R["x2"], dst=R["out"], r_lo=0, r_hi=HQ)
        self.barrier()
        self.interleave(self.p_experts(dst=R["out"], q_lo=0, q_hi=HQ),
                        self.p4a(ts_lo=HTS, ts_hi=NTS))
        self.barrier()
        self.p4b(dst=R["x2"], ts_lo=HTS, ts_hi=NTS)
        self.barrier()
        self.p_ln_router(src=R["x2"], q_lo=HQ, q_hi=NQ)
        self.barrier()
        self.copy_resid(src=R["x2"], dst=R["out"], r_lo=HQ, r_hi=NQ)
        self.barrier()
        self.drain(self.p_experts(dst=R["out"], q_lo=HQ, q_hi=NQ))
        self.barrier()
        self.epilogue()

    def drain(self, gen):
        for _ in gen:
            pass

    def interleave(self, ga, gb, nb=1):
        alive_a = alive_b = True
        while alive_a or alive_b:
            if alive_a:
                try:
                    next(ga)
                except StopIteration:
                    alive_a = False
            for _ in range(nb):
                if alive_b:
                    try:
                        next(gb)
                    except StopIteration:
                        alive_b = False

    def vf(self, ins):
        C = self.C
        C["v"].inc(ins)
        self.v.wait_ge(C["v"].h, C["v"].v)

    def sf(self, ins):
        C = self.C
        C["s"].inc(ins)
        self.s.wait_ge(C["s"].h, C["s"].v)

    def barrier(self):
        snap = {"t": self.C["t"].v, "v": self.C["v"].v,
                "s": self.C["s"].v, "g": self.C["g"].v}
        own = {"tensor": "t", "vector": "v", "scalar": "s", "gpsimd": "g"}
        for enm in ENGINES:
            e = self.E[enm]
            for om, val in snap.items():
                if val > 0 and om != own.get(enm):
                    e.wait_ge(self.C[om].h, val)

    def epilogue(self):
        C = self.C
        self.g.wait_ge(C["d_acc"].h, C["d_acc"].v)
        for nm in ("d_st0", "d_st1", "d_cp"):
            self.sy.wait_ge(C[nm].h, C[nm].v)
        self.barrier()

    def prologue(self):
        R, C = self.R, self.C
        t, v, s, sy, g = self.t, self.v, self.s, self.sy, self.g
        wi = R["wi"]
        g.memset(R["IDF"][:], 0.0)
        g.affine_select(out=R["IDF"][:], in_=R["IDF"][:],
                        compare_op=ALU.not_equal, fill=1.0, base=0,
                        pattern=[[-1, 128]], channel_multiplier=1)
        g.memset(R["IDB"][:], 0.0)
        g.affine_select(out=R["IDB"][:], in_=R["IDB"][:],
                        compare_op=ALU.not_equal, fill=1.0, base=0,
                        pattern=[[-1, 128]], channel_multiplier=1)
        g.memset(R["WRSB"][:], 0.0)
        g.memset(R["EPSC"][:], EPS)
        ins = g.memset(R["BIASR"][:], 0.0)
        C["g"].inc(ins)
        g.wait_ge(C["g"].h, C["g"].v)
        ins = g.dma_start(
            out=R["WRSB"][:].rearrange("p (kb e) -> p kb e", e=8)[:, :, 0:7],
            in_=wi["Wr"].rearrange("(kb p) e -> p kb e", p=128))
        C["d_w0"].inc(ins, 16)
        ins = g.dma_start(out=R["BIASR"][:, 0:7],
                          in_=wi["r_bias"].unsqueeze(0)
                          .partition_broadcast(128).squeeze(1))
        C["d_w0"].inc(ins, 16)
        ins = g.dma_start(
            out=R["ASC"][:].rearrange("p (hb s) -> p hb s", s=NS),
            in_=wi["A_log"].rearrange("(hb p) s -> p hb s", p=128))
        C["d_w0"].inc(ins, 16)
        ins = g.dma_start(out=R["BDT"][:],
                          in_=wi["b_dt"].rearrange("(hb p) -> p hb", p=128),
                          allow_slow_non_contiguous=True)
        C["d_w0"].inc(ins, 16)
        ins = g.dma_start(
            out=R["WBC"][:, 0:KH * NS].rearrange("p (kb s) -> p kb s", s=NS),
            in_=wi["W_B"].rearrange("(kb p) s -> p kb s", p=128))
        C["d_w0"].inc(ins, 16)
        ins = g.dma_start(
            out=R["WBC"][:, KH * NS:2 * KH * NS]
            .rearrange("p (kb s) -> p kb s", s=NS),
            in_=wi["W_C"].rearrange("(kb p) s -> p kb s", p=128))
        C["d_w0"].inc(ins, 16)
        s.wait_ge(C["d_w0"].h, C["d_w0"].v)
        ins = s.activation(out=R["ASC"][:], in_=R["ASC"][:], func=AF.Exp)
        C["s"].inc(ins)
        v.wait_ge(C["s"].h, C["s"].v)
        v.tensor_scalar(out=R["ASC"][:], in0=R["ASC"][:], scalar1=1e-6,
                        scalar2=1.0, op0=ALU.max, op1=ALU.min)
        ins = v.tensor_scalar(out=R["ASC"][:], in0=R["ASC"][:], scalar1=-1.0,
                              scalar2=None, op0=ALU.mult)
        C["v"].inc(ins)
        ins = v.memset(R["HST"][:], 0.0)
        C["v"].inc(ins)
        for enm in ENGINES:
            self.E[enm].wait_ge(C["v"].h, C["v"].v)
            self.E[enm].wait_ge(C["g"].h, C["g"].v)
            self.E[enm].wait_ge(C["d_w0"].h, C["d_w0"].v)

    def dma_gate(self, *sems):
        C = self.C
        for nm in sems:
            self.sy.wait_ge(C[nm].h, C[nm].v)
            self.g.wait_ge(C[nm].h, C[nm].v)

    def copy_resid(self, src, dst, r_lo=0, r_hi=NQ):
        C = self.C
        self.dma_gate("d_acc", "d_st0", "d_st1")
        ins = self.sy.dma_start(out=dst[r_lo * 128:r_hi * 128, :],
                                in_=src[r_lo * 128:r_hi * 128, :])
        C["d_cp"].inc(ins, 16)
        self.g.wait_ge(C["d_cp"].h, C["d_cp"].v)
        self.sy.wait_ge(C["d_cp"].h, C["d_cp"].v)

    def _ln_tile(self, xt, dst32):
        R, C = self.R, self.C
        s, v = self.s, self.v
        STr = R["STT"]
        s.activation(out=R["DUM"][:], in_=xt, func=AF.Copy,
                     accum_out=STr[:, 0:1])
        ins = s.activation(out=R["DUM"][:], in_=xt, func=AF.Square,
                           accum_out=STr[:, 1:2])
        self.sf(ins)
        s.mul(out=STr[:, 2:3], in_=STr[:, 0:1], mul=1.0 / D)
        ins = s.mul(out=STr[:, 3:4], in_=STr[:, 1:2], mul=1.0 / D)
        self.sf(ins)
        ins = s.activation(out=STr[:, 4:5], in_=STr[:, 2:3], func=AF.Square)
        C["s"].inc(ins)
        v.wait_ge(C["s"].h, C["s"].v)
        ins = v.tensor_sub(STr[:, 5:6], STr[:, 3:4], STr[:, 4:5])
        C["v"].inc(ins)
        s.wait_ge(C["v"].h, C["v"].v)
        ins = s.activation(out=STr[:, 6:7], in_=STr[:, 5:6], func=AF.Ln,
                           bias=R["EPSC"][:, 0:1])
        self.sf(ins)
        ins = s.activation(out=STr[:, 7:8], in_=STr[:, 6:7], func=AF.Exp,
                           scale=-0.5)
        C["s"].inc(ins)
        v.wait_ge(C["s"].h, C["s"].v)
        ins = v.tensor_scalar(out=dst32, in0=xt, scalar1=STr[:, 2:3],
                              scalar2=STr[:, 7:8], op0=ALU.subtract,
                              op1=ALU.mult)
        return C["v"].inc(ins)

    def p_ln_router(self, src, q_lo, q_hi):
        R, C = self.R, self.C
        t, v, s, sy, g = self.t, self.v, self.s, self.sy, self.g
        self.dma_gate("d_acc")
        dpar = [C["d_a0"], C["d_a1"]]
        spar = [C["d_st0"], C["d_st1"]]
        STr = R["STT"]
        for q in range(q_lo, q_hi):
            par = q % 2
            xt = R["XTI"][:, par * D:(par + 1) * D]
            sy.wait_ge(C["v"].h, self.h_xti[par])
            sy.wait_ge(C["s"].h, self.h_xtis[par])
            ins = sy.dma_start(out=xt, in_=src[q * 128:(q + 1) * 128, :])
            wv = dpar[par].inc(ins, 16)
            s.wait_ge(C["v"].h, self.h_p1v)
            s.wait_ge(dpar[par].h, wv)
            v.wait_ge(C["t"].h, self.h_trt)
            wv_xn = self._ln_tile(xt, R["XN32"][:])
            self.h_xtis[par] = C["s"].v
            t.wait_ge(C["v"].h, wv_xn)
            t.wait_ge(C["s"].h, self.h_p1ev)
            ins = None
            for k in range(KD):
                ins = t.transpose(R["pa"][3][:, k * 128:(k + 1) * 128],
                                  R["XN32"][:, k * 128:(k + 1) * 128],
                                  R["IDF"][:])
            C["t"].inc(ins)
            self.h_trt = C["t"].v
            s.wait_ge(C["t"].h, C["t"].v)
            s.wait_ge(spar[par].h, self.h_stg[par])
            stg = R["STG"][:, par * D:(par + 1) * D]
            s.copy(out=R["XNT32"][:], in_=R["pa"][3][:])
            ins = s.copy(out=stg, in_=R["pa"][3][:])
            self.h_p1ev = C["s"].inc(ins)
            sy.wait_ge(C["s"].h, self.h_p1ev)
            ins = sy.dma_start(
                out=R["xnbT"].rearrange("(kb p) t -> p kb t", p=128)
                [:, :, q * 128:(q + 1) * 128],
                in_=stg.rearrange("p (kb t) -> p kb t", t=128))
            self.h_stg[par] = spar[par].inc(ins, 16)
            t.wait_ge(C["v"].h, self.h_routerv)
            t.wait_ge(C["s"].h, self.h_p1ev)
            ins = None
            for kb in range(KD):
                ins = t.matmul(R["pa"][2][:, 0:8],
                               R["XNT32"][:, kb * 128:(kb + 1) * 128],
                               R["WRSB"][:, kb * 8:(kb + 1) * 8],
                               start=(kb == 0), stop=(kb == KD - 1))
            C["t"].inc(ins)
            v.wait_ge(C["t"].h, C["t"].v)
            PL = STr[:, 8:16]
            ins = v.tensor_mul(PL, R["pa"][2][:, 0:8], R["BIASR"][:])
            self.h_routerv = C["v"].inc(ins)
            s.wait_ge(C["v"].h, self.h_routerv)
            ins = s.activation(out=PL, in_=PL, func=AF.Sigmoid)
            C["s"].inc(ins)
            v.wait_ge(C["s"].h, C["s"].v)
            MK = R["DUM"][:, 0:8]
            P2 = R["DUM"][:, 8:16]
            HUGE = 1.0e30
            self.vf(v.memset(STr[:, 15:16], -1.0))
            self.vf(v.tensor_reduce(STr[:, 16:17], PL, axis=AXX, op=ALU.max))
            self.vf(v.tensor_scalar(out=MK, in0=PL, scalar1=STr[:, 16:17],
                                    scalar2=-HUGE, op0=ALU.subtract,
                                    op1=ALU.mult))
            self.vf(v.tensor_scalar(out=MK, in0=MK, scalar1=1.0, scalar2=0.0,
                                    op0=ALU.min, op1=ALU.max))
            self.vf(v.tensor_mul(P2, PL, MK))
            self.vf(v.tensor_reduce(STr[:, 17:18], P2, axis=AXX, op=ALU.max))
            self.vf(v.tensor_add(STr[:, 18:19], STr[:, 16:17],
                                 STr[:, 17:18]))
            self.vf(v.reciprocal(STr[:, 19:20], STr[:, 18:19]))
            self.vf(v.tensor_scalar(out=MK, in0=PL, scalar1=STr[:, 17:18],
                                    scalar2=-HUGE, op0=ALU.subtract,
                                    op1=ALU.mult))
            self.vf(v.tensor_scalar(out=MK, in0=MK, scalar1=1.0, scalar2=0.0,
                                    op0=ALU.min, op1=ALU.max))
            self.vf(v.tensor_mul(MK, PL, MK))
            self.vf(v.tensor_sub(P2, PL, MK))
            ins = v.tensor_scalar(out=R["WSB"][:, q * 8:(q + 1) * 8],
                                  in0=P2, scalar1=STr[:, 19:20], scalar2=0.5,
                                  op0=ALU.mult, op1=ALU.mult)
            self.h_p1v = C["v"].inc(ins)
            self.h_xti[par] = self.h_p1v

    def _wviews(self, par, n):
        base = par * 18432
        W = self.R["POOLW"]
        wg = W[:, base:base + KD * n * 128]
        wu = W[:, base + 6144:base + 6144 + KD * n * 128]
        wd = W[:, base + 12288:base + 12288 + n * D]
        return wg, wu, wd

    def p_experts(self, dst, q_lo, q_hi):
        R, C = self.R, self.C
        t, v, s, sy, g = self.t, self.v, self.s, self.sy, self.g
        self.dma_gate("d_st0", "d_st1", "d_cp")
        wi = R["wi"]
        tb_lo, tb_hi = q_lo // 4, q_hi // 4
        wpar = [C["d_w0"], C["d_w1"]]
        xpar = [C["d_b0"], C["d_b1"]]
        steps = [(e, ci) for e in range(8) for ci in range(len(CHUNKS))]

        def issue_chunk(i):
            if i >= len(steps):
                return
            e, ci = steps[i]
            par = i % 2
            lo, hi = CHUNKS[ci]
            n = hi - lo
            g.wait_ge(C["t"].h, self.h_wchunk[par])
            wg, wu, wd = self._wviews(par, n)
            if e == 0:
                gs, us, ds = wi["sh_gate"], wi["sh_up"], wi["sh_down"]
            else:
                gs, us, ds = (wi["r_gate"][e - 1], wi["r_up"][e - 1],
                              wi["r_down"][e - 1])
            ins = g.dma_start(
                out=wg.rearrange("p (kb h) -> p kb h", kb=KD),
                in_=gs.rearrange("(kb p) h -> p kb h", p=128)
                [:, :, lo * 128:hi * 128])
            wpar[par].inc(ins, 16)
            ins = g.dma_start(
                out=wu.rearrange("p (kb h) -> p kb h", kb=KD),
                in_=us.rearrange("(kb p) h -> p kb h", p=128)
                [:, :, lo * 128:hi * 128])
            wpar[par].inc(ins, 16)
            ins = g.dma_start(
                out=wd.rearrange("p (hb d) -> p hb d", hb=n),
                in_=ds.rearrange("(hb p) d -> p hb d", p=128)[:, lo:hi, :])
            self._wtarget[i] = wpar[par].inc(ins, 16)

        issue_chunk(0)
        issue_chunk(1)

        for i, (e, ci) in enumerate(steps):
            par = i % 2
            lo, hi = CHUNKS[ci]
            n = hi - lo
            wg, wu, wd = self._wviews(par, n)
            t.wait_ge(wpar[par].h, self._wtarget[i])
            for tb in range(tb_lo, tb_hi):
                xp = tb % 2
                xnt = R["XNT"][:, xp * KD * TBF:(xp + 1) * KD * TBF]
                sy.wait_ge(C["t"].h, self.h_xnt[xp])
                ins = sy.dma_start(
                    out=xnt.rearrange("p (kb t) -> p kb t", kb=KD),
                    in_=R["xnbT"].rearrange("(kb p) t -> p kb t", p=128)
                    [:, :, tb * TBF:(tb + 1) * TBF])
                wv_x = xpar[xp].inc(ins, 16)
                t.wait_ge(xpar[xp].h, wv_x)
                for j in range(n):
                    pp = j % 2
                    psg = R["pa"][0][:, pp * 512:(pp + 1) * 512]
                    psu = R["pa"][1][:, pp * 512:(pp + 1) * 512]
                    t.wait_ge(C["s"].h, self.h_silu[pp])
                    t.wait_ge(C["v"].h, self.h_hmul[pp])
                    ins = None
                    for kb in range(KD):
                        ins = t.matmul(psg,
                                       wg[:, (kb * n + j) * 128:
                                          (kb * n + j + 1) * 128],
                                       xnt[:, kb * TBF:(kb + 1) * TBF],
                                       start=(kb == 0), stop=(kb == KD - 1))
                    C["t"].inc(ins)
                    for kb in range(KD):
                        ins = t.matmul(psu,
                                       wu[:, (kb * n + j) * 128:
                                          (kb * n + j + 1) * 128],
                                       xnt[:, kb * TBF:(kb + 1) * TBF],
                                       start=(kb == 0), stop=(kb == KD - 1))
                    C["t"].inc(ins)
                    s.wait_ge(C["t"].h, C["t"].v)
                    hs = R["HSIL"][:, pp * TBF:(pp + 1) * TBF]
                    ins = s.activation(out=hs, in_=psg, func=AF.Silu)
                    self.h_silu[pp] = C["s"].inc(ins)
                    v.wait_ge(C["s"].h, self.h_silu[pp])
                    ins = v.tensor_mul(R["HT"][:, j * TBF:(j + 1) * TBF],
                                       hs, psu)
                    self.h_hmul[pp] = C["v"].inc(ins)
                self.h_xnt[xp] = C["t"].v
                yield
                t.wait_ge(C["v"].h, self.h_hmul[0])
                t.wait_ge(C["v"].h, self.h_hmul[1])
                for tq in range(4):
                    q = tb * 4 + tq
                    app = q % 2
                    ps = R["pa"][2 + app]
                    t.wait_ge(C["s"].h, self.h_ev[app])
                    ins = None
                    for dh in range(2):
                        for j in range(n):
                            ins = t.matmul(
                                ps[:, dh * 512:(dh + 1) * 512],
                                R["HT"][:, j * TBF + tq * 128:
                                        j * TBF + tq * 128 + 128],
                                wd[:, j * D + dh * 512:j * D + dh * 512 + 512],
                                start=(j == 0), stop=(j == n - 1))
                    C["t"].inc(ins)
                    s.wait_ge(C["t"].h, C["t"].v)
                    s.wait_ge(C["d_acc"].h, self.h_acc[app])
                    ev = R["EV"][:, app * D:(app + 1) * D]
                    if e == 0:
                        ins = s.activation(out=ev, in_=ps[:], func=AF.Copy,
                                           scale=0.5)
                    else:
                        ins = s.activation(out=ev, in_=ps[:], func=AF.Copy,
                                           scale=R["WSB"][:, q * 8 + e - 1:
                                                          q * 8 + e])
                    self.h_ev[app] = C["s"].inc(ins)
                    g.wait_ge(C["s"].h, self.h_ev[app])
                    ins = g.dma_start(out=dst[q * 128:(q + 1) * 128, :],
                                      in_=ev, accum_op=ALU.add)
                    self.h_acc[app] = C["d_acc"].inc(ins, 16)
                yield
            self.h_wchunk[par] = C["t"].v
            issue_chunk(i + 2)

    def p3a(self, src, tb_lo, tb_hi):
        R, C = self.R, self.C
        t, v, s, sy, g = self.t, self.v, self.s, self.sy, self.g
        self.dma_gate("d_acc")
        wi = R["wi"]
        W = R["POOLW"]
        w_in = W[:, 0:KD * DI]
        w_gt = W[:, KD * DI:2 * KD * DI]
        ins = g.dma_start(out=w_in.rearrange("p (kb h) -> p kb h", kb=KD),
                          in_=wi["W_in"].rearrange("(kb p) h -> p kb h",
                                                   p=128))
        C["d_w0"].inc(ins, 16)
        ins = g.dma_start(out=w_gt.rearrange("p (kb h) -> p kb h", kb=KD),
                          in_=wi["W_gate"].rearrange("(kb p) h -> p kb h",
                                                     p=128))
        C["d_w0"].inc(ins, 16)
        M = R["POOLM"]
        nwr = M[:, 0:2048].bitcast(F32)
        nbr = M[:, 2048:4096].bitcast(F32)
        ins = g.dma_start(out=nwr, in_=wi["norm_w"].unsqueeze(0)
                          .partition_broadcast(128).squeeze(1))
        C["d_bc0"].inc(ins, 16)
        ins = g.dma_start(out=nbr, in_=wi["norm_b"].unsqueeze(0)
                          .partition_broadcast(128).squeeze(1))
        C["d_bc0"].inc(ins, 16)
        v.wait_ge(C["d_bc0"].h, C["d_bc0"].v)
        t.wait_ge(C["d_w0"].h, C["d_w0"].v)

        dpar = [C["d_a0"], C["d_a1"]]
        zt_stage = M[:, 4096:4096 + KH * TB3]
        xnm_bf = M[:, 4096 + KH * TB3:4096 + KH * TB3 + D]
        wv_mm = 0
        for tb in range(tb_lo, tb_hi):
            for qq in range(4):
                q = tb * 4 + qq
                par = q % 2
                xt = R["XTI"][:, par * D:(par + 1) * D]
                sy.wait_ge(C["v"].h, self.h_xti[par])
                sy.wait_ge(C["s"].h, self.h_xtis[par])
                ins = sy.dma_start(out=xt, in_=src[q * 128:(q + 1) * 128, :])
                wv = dpar[par].inc(ins, 16)
                s.wait_ge(C["v"].h, self.h_p3v)
                s.wait_ge(dpar[par].h, wv)
                v.wait_ge(C["t"].h, self.h_trt)
                self._ln_tile(xt, R["XN32"][:])
                v.wait_ge(C["d_st0"].h, self.h_x2st[par])
                ev = R["EV"][:, par * D:(par + 1) * D]
                ins = v.tensor_add(ev, xt, R["XN32"][:])
                wv2 = C["v"].inc(ins)
                sy.wait_ge(C["v"].h, wv2)
                ins = sy.dma_start(out=R["x2"][q * 128:(q + 1) * 128, :],
                                   in_=ev)
                self.h_x2st[par] = C["d_st0"].inc(ins, 16)
                s.wait_ge(C["v"].h, wv2)
                self._ln_tile(R["XN32"][:], R["XN32"][:])
                self.h_xtis[par] = C["s"].v
                ins = v.tensor_mul(R["XN32"][:], R["XN32"][:], nwr)
                ins = v.tensor_add(xnm_bf, R["XN32"][:], nbr)
                wv3 = C["v"].inc(ins)
                self.h_p3v = wv3
                self.h_xti[par] = wv2
                t.wait_ge(C["v"].h, wv3)
                t.wait_ge(C["s"].h, self.h_trev)
                pab = R["pa"][3][:].bitcast(BF16)
                ins = None
                for k in range(KD):
                    ins = t.transpose(pab[:, k * 128:(k + 1) * 128],
                                      xnm_bf[:, k * 128:(k + 1) * 128],
                                      R["IDB"][:])
                C["t"].inc(ins)
                self.h_trt = C["t"].v
                s.wait_ge(C["t"].h, C["t"].v)
                if qq == 0:
                    s.wait_ge(C["t"].h, wv_mm)
                xnt = R["XNT"][:, 0:KD * TB3]
                ins = s.copy(
                    out=xnt.rearrange("p (kb t) -> p kb t", t=TB3)
                    [:, :, qq * 128:(qq + 1) * 128],
                    in_=pab[:, 0:KD * 128].rearrange("p (kb t) -> p kb t",
                                                     t=128))
                self.h_trev = C["s"].inc(ins)
            t.wait_ge(C["s"].h, self.h_trev)
            for hb in range(KH):
                pp = hb % 2
                psz = R["pa"][0][:, pp * 512:(pp + 1) * 512]
                psgt = R["pa"][1][:, pp * 512:(pp + 1) * 512]
                t.wait_ge(C["s"].h, self.h_zev[pp])
                ins = None
                for kb in range(KD):
                    ins = t.matmul(psz,
                                   w_in[:, kb * DI + hb * 128:
                                        kb * DI + (hb + 1) * 128],
                                   xnt[:, kb * TB3:(kb + 1) * TB3],
                                   start=(kb == 0), stop=(kb == KD - 1))
                C["t"].inc(ins)
                for kb in range(KD):
                    ins = t.matmul(psgt,
                                   w_gt[:, kb * DI + hb * 128:
                                        kb * DI + (hb + 1) * 128],
                                   xnt[:, kb * TB3:(kb + 1) * TB3],
                                   start=(kb == 0), stop=(kb == KD - 1))
                C["t"].inc(ins)
                wv_mm = C["t"].v
                s.wait_ge(C["t"].h, C["t"].v)
                if hb == 0:
                    s.wait_ge(C["d_st0"].h, self.h_zst)
                s.wait_ge(C["d_st1"].h, self.h_gst[pp])
                s.copy(out=zt_stage[:, hb * TB3:(hb + 1) * TB3], in_=psz)
                gstage = R["HSIL"][:, pp * TB3:(pp + 1) * TB3]
                ins = s.activation(out=gstage, in_=psgt, func=AF.Silu)
                self.h_zev[pp] = C["s"].inc(ins)
                sy.wait_ge(C["s"].h, self.h_zev[pp])
                ins = sy.dma_start(
                    out=R["gateh"][hb * 128:(hb + 1) * 128,
                                   tb * TB3:(tb + 1) * TB3], in_=gstage)
                self.h_gst[pp] = C["d_st1"].inc(ins, 16)
            sy.wait_ge(C["s"].h, self.h_zev[0])
            sy.wait_ge(C["s"].h, self.h_zev[1])
            ins = sy.dma_start(
                out=R["zth"].rearrange("(hb p) t -> p hb t", p=128)
                [:, :, tb * TB3:(tb + 1) * TB3],
                in_=zt_stage.rearrange("p (hb t) -> p hb t", hb=KH))
            self.h_zst = C["d_st0"].inc(ins, 16)

    def p3b(self, tb_lo, tb_hi):
        R, C = self.R, self.C
        t, v, s, sy, g = self.t, self.v, self.s, self.sy, self.g
        self.dma_gate("d_st0")
        wi = R["wi"]
        W = R["POOLW"]
        w_dt = W[:, 0:KH * DI]
        ins = g.dma_start(out=w_dt.rearrange("p (kb h) -> p kb h", kb=KH),
                          in_=wi["W_dt"].rearrange("(kb p) h -> p kb h",
                                                   p=128))
        C["d_w0"].inc(ins, 16)
        t.wait_ge(C["d_w0"].h, C["d_w0"].v)
        M = R["POOLM"]
        zt = [M[:, 0:KH * TB3], M[:, KH * TB3:2 * KH * TB3]]
        o = 2 * KH * TB3
        dts = M[:, o:o + 2 * TB3]; o += 2 * TB3
        dtzs = M[:, o:o + 2 * TB3]; o += 2 * TB3
        scr = M[:, o:o + 4096].bitcast(F32); o += 4096
        bcs = M[:, o:o + 1024]; o += 1024
        zpar = [C["d_a0"], C["d_a1"]]
        zused = [0, 0]
        for tb in range(tb_lo, tb_hi):
            par = tb % 2
            sy.wait_ge(C["t"].h, zused[par])
            sy.wait_ge(C["v"].h, self.h_dtv[par])
            ins = sy.dma_start(
                out=zt[par].rearrange("p (hb t) -> p hb t", hb=KH),
                in_=R["zth"].rearrange("(hb p) t -> p hb t", p=128)
                [:, :, tb * TB3:(tb + 1) * TB3])
            wv = zpar[par].inc(ins, 16)
            t.wait_ge(zpar[par].h, wv)
            v.wait_ge(zpar[par].h, wv)
            for hb in range(KH):
                pp = hb % 2
                psd = R["pa"][0][:, pp * 512:(pp + 1) * 512]
                t.wait_ge(C["s"].h, self.h_dA[pp])
                ins = None
                for kb in range(KH):
                    ins = t.matmul(psd,
                                   w_dt[:, kb * DI + hb * 128:
                                        kb * DI + (hb + 1) * 128],
                                   zt[par][:, kb * TB3:(kb + 1) * TB3],
                                   start=(kb == 0), stop=(kb == KH - 1))
                C["t"].inc(ins)
                s.wait_ge(C["t"].h, C["t"].v)
                s.wait_ge(C["v"].h, self.h_dtv[pp])
                bcol = R["BDT"][:, hb:hb + 1]
                sa = scr[:, pp * 1024:pp * 1024 + 512]
                sbx = scr[:, pp * 1024 + 512:(pp + 1) * 1024]
                s.activation(out=sa, in_=psd, func=AF.Abs, bias=bcol)
                s.activation(out=sa, in_=sa, func=AF.Exp, scale=-1.0)
                s.activation(out=sa, in_=sa, func=AF.Ln, bias=1.0)
                ins = s.activation(out=sbx, in_=psd, func=AF.Relu, bias=bcol)
                self.h_dA[pp] = C["s"].inc(ins)
                v.wait_ge(C["s"].h, self.h_dA[pp])
                v.wait_ge(C["d_st0"].h, self.h_dtst[pp])
                dtt = dts[:, pp * TB3:(pp + 1) * TB3]
                dtzt = dtzs[:, pp * TB3:(pp + 1) * TB3]
                v.tensor_add(sa, sa, sbx)
                v.tensor_scalar(out=dtt, in0=sa, scalar1=0.001, scalar2=1.0,
                                op0=ALU.max, op1=ALU.min)
                ins = v.tensor_mul(dtzt, dtt,
                                   zt[par][:, hb * TB3:(hb + 1) * TB3])
                self.h_dtv[pp] = C["v"].inc(ins)
                sy.wait_ge(C["v"].h, self.h_dtv[pp])
                ins = sy.dma_start(
                    out=R["dth"][hb * 128:(hb + 1) * 128,
                                 tb * TB3:(tb + 1) * TB3], in_=dtt)
                C["d_st0"].inc(ins, 16)
                ins = sy.dma_start(
                    out=R["dtzh"][hb * 128:(hb + 1) * 128,
                                  tb * TB3:(tb + 1) * TB3], in_=dtzt)
                self.h_dtst[pp] = C["d_st0"].inc(ins, 16)
            zused[par] = C["t"].v
            t.wait_ge(C["s"].h, self.h_bcev)
            ins = None
            for kb in range(KH):
                ins = t.matmul(R["pa"][2][0:16, 0:512],
                               R["WBC"][:, kb * NS:(kb + 1) * NS],
                               zt[par][:, kb * TB3:(kb + 1) * TB3],
                               start=(kb == 0), stop=(kb == KH - 1))
            C["t"].inc(ins)
            for kb in range(KH):
                ins = t.matmul(R["pa"][2][0:16, 512:1024],
                               R["WBC"][:, KH * NS + kb * NS:
                                        KH * NS + (kb + 1) * NS],
                               zt[par][:, kb * TB3:(kb + 1) * TB3],
                               start=(kb == 0), stop=(kb == KH - 1))
            C["t"].inc(ins)
            zused[par] = C["t"].v
            s.wait_ge(C["t"].h, C["t"].v)
            s.wait_ge(C["d_st1"].h, self.h_gst[0])
            s.copy(out=bcs[0:16, 0:512], in_=R["pa"][2][0:16, 0:512])
            ins = s.copy(out=bcs[0:16, 512:1024],
                         in_=R["pa"][2][0:16, 512:1024])
            self.h_bcev = C["s"].inc(ins)
            sy.wait_ge(C["s"].h, self.h_bcev)
            ins = sy.dma_start(out=R["bch"][:, 0, tb * TB3:(tb + 1) * TB3],
                               in_=bcs[0:16, 0:512])
            C["d_st1"].inc(ins, 16)
            ins = sy.dma_start(out=R["bch"][:, 1, tb * TB3:(tb + 1) * TB3],
                               in_=bcs[0:16, 512:1024])
            self.h_gst[0] = C["d_st1"].inc(ins, 16)

    def p4a(self, ts_lo, ts_hi):
        R, C = self.R, self.C
        t, v, s, sy, g = self.t, self.v, self.s, self.sy, self.g
        self.dma_gate("d_st0", "d_st1")
        M = R["POOLM"]
        o = 0
        dtw = [None, None]
        dtzw = [None, None]
        for pq in range(2):
            dtw[pq] = M[:, o:o + KH * TSC]; o += KH * TSC
            dtzw[pq] = M[:, o:o + KH * TSC]; o += KH * TSC
        yac = M[:, o:o + KH * TSC]; o += KH * TSC
        bbc = M[:, o:o + 4 * TSC]; o += 4 * TSC
        scrA = M[:, o:o + 8 * TSC].bitcast(F32); o += 8 * TSC
        hhb = M[:, o:o + KH * TSC]; o += KH * TSC
        scrB = M[:, o:o + 4 * TSC]; o += 4 * TSC
        ldp = [C["d_a0"], C["d_a1"]]
        bcp = [C["d_bc0"], C["d_bc1"]]
        st = C["d_st0"]
        for ts in range(ts_lo, ts_hi):
            par = ts % 2
            sy.wait_ge(C["s"].h, self.h_dtuse_s[par])
            sy.wait_ge(C["v"].h, self.h_dtuse_v[par])
            ins = sy.dma_start(
                out=dtw[par].rearrange("p (hb t) -> p hb t", hb=KH),
                in_=R["dth"].rearrange("(hb p) t -> p hb t", p=128)
                [:, :, ts * TSC:(ts + 1) * TSC])
            ldp[par].inc(ins, 16)
            ins = sy.dma_start(
                out=dtzw[par].rearrange("p (hb t) -> p hb t", hb=KH),
                in_=R["dtzh"].rearrange("(hb p) t -> p hb t", p=128)
                [:, :, ts * TSC:(ts + 1) * TSC])
            wv_ld = ldp[par].inc(ins, 16)
            s.wait_ge(ldp[par].h, wv_ld)
            v.wait_ge(ldp[par].h, wv_ld)
            v.wait_ge(st.h, self.h_ygst)
            for si in range(NS):
                bp = si % 2
                g.wait_ge(C["v"].h, self.h_sdone[bp])
                ins = g.dma_start(
                    out=bbc[:, bp * 512:(bp + 1) * 512]
                    .rearrange("p (bc t) -> p bc t", bc=2),
                    in_=R["bch"][si:si + 1, :, ts * TSC:(ts + 1) * TSC]
                    .partition_broadcast(128).squeeze(1))
                wv_bc = bcp[bp].inc(ins, 16)
                v.wait_ge(bcp[bp].h, wv_bc)
                Bbc = bbc[:, bp * 512:bp * 512 + 256]
                Cbc = bbc[:, bp * 512 + 256:(bp + 1) * 512]
                for half in range(2):
                  hbs = range(half * 8, half * 8 + 8)
                  for hb in hbs:
                    ip = hb % 4
                    dA = scrA[:, ip * 256:(ip + 1) * 256]
                    uu = scrB[:, (hb % 2) * 256:(hb % 2) * 256 + 256]
                    hh = hhb[:, hb * TSC:(hb + 1) * TSC]
                    s.wait_ge(C["v"].h, self.h_scan[ip])
                    acol = R["ASC"][:, hb * NS + si:hb * NS + si + 1]
                    ins = s.activation(
                        out=dA, in_=dtw[par][:, hb * TSC:(hb + 1) * TSC],
                        func=AF.Exp, scale=acol)
                    self.h_dA[ip] = C["s"].inc(ins)
                    v.wait_ge(C["s"].h, self.h_dA[ip])
                    hcol = R["HST"][:, hb * NS + si:hb * NS + si + 1]
                    v.tensor_mul(uu, dtzw[par][:, hb * TSC:(hb + 1) * TSC],
                                 Bbc)
                    ins = v.tensor_tensor_scan(out=hh, data0=dA, data1=uu,
                                               initial=hcol, op0=ALU.mult,
                                               op1=ALU.add)
                    self.h_scan[ip] = C["v"].inc(ins)
                  v.wait_ge(C["v"].h, C["v"].v)
                  for hb in hbs:
                    hcol = R["HST"][:, hb * NS + si:hb * NS + si + 1]
                    v.tensor_copy(out=hcol,
                                  in_=hhb[:, (hb + 1) * TSC - 1:
                                          (hb + 1) * TSC])
                  for hb in hbs:
                    hh = hhb[:, hb * TSC:(hb + 1) * TSC]
                    hc = scrB[:, 512 + (hb % 2) * 256:512 + (hb % 2) * 256
                              + 256]
                    if si == 0:
                        ins = v.tensor_mul(
                            yac[:, hb * TSC:(hb + 1) * TSC], hh, Cbc)
                    else:
                        ins = v.tensor_mul(hc, hh, Cbc)
                        ins = v.tensor_tensor(
                            out=yac[:, hb * TSC:(hb + 1) * TSC],
                            in0=yac[:, hb * TSC:(hb + 1) * TSC],
                            in1=hc, op=ALU.add)
                  C["v"].inc(ins)
                  self.h_sdone[bp] = C["v"].v
                  yield
            self.h_dtuse_s[par] = C["s"].v
            self.h_dtuse_v[par] = C["v"].v
            sy.wait_ge(C["v"].h, C["v"].v)
            ins = sy.dma_start(
                out=R["zth"].rearrange("(hb p) t -> p hb t", p=128)
                [:, :, ts * TSC:(ts + 1) * TSC],
                in_=yac.rearrange("p (hb t) -> p hb t", hb=KH))
            self.h_ygst = st.inc(ins, 16)

    def p4b(self, dst, ts_lo, ts_hi):
        R, C = self.R, self.C
        t, v, s, sy, g = self.t, self.v, self.s, self.sy, self.g
        self.dma_gate("d_st0", "d_st1", "d_cp")
        W = R["POOLW"]
        w_out = W[:, 0:KH * D]
        ins = g.dma_start(out=w_out.rearrange("p (kb d) -> p kb d", kb=KH),
                          in_=R["wi"]["W_out"]
                          .rearrange("(kb p) d -> p kb d", p=128))
        C["d_w1"].inc(ins, 16)
        t.wait_ge(C["d_w1"].h, C["d_w1"].v)
        M = R["POOLM"]
        ya = [M[:, 0:KH * TSC], M[:, KH * TSC:2 * KH * TSC]]
        gt = [M[:, 2 * KH * TSC:3 * KH * TSC], M[:, 3 * KH * TSC:4 * KH * TSC]]
        ldp = [C["d_b0"], C["d_b1"]]
        for ts in range(ts_lo, ts_hi):
            par = ts % 2
            sy.wait_ge(C["t"].h, self.h_p4bmm[par])
            sy.wait_ge(C["v"].h, self.h_p4bv[par])
            ins = sy.dma_start(
                out=ya[par].rearrange("p (hb t) -> p hb t", hb=KH),
                in_=R["zth"].rearrange("(hb p) t -> p hb t", p=128)
                [:, :, ts * TSC:(ts + 1) * TSC])
            ldp[par].inc(ins, 16)
            ins = sy.dma_start(
                out=gt[par].rearrange("p (hb t) -> p hb t", hb=KH),
                in_=R["gateh"].rearrange("(hb p) t -> p hb t", p=128)
                [:, :, ts * TSC:(ts + 1) * TSC])
            wv = ldp[par].inc(ins, 16)
            v.wait_ge(ldp[par].h, wv)
            ins = None
            for hb in range(KH):
                ins = v.tensor_mul(ya[par][:, hb * TSC:(hb + 1) * TSC],
                                   ya[par][:, hb * TSC:(hb + 1) * TSC],
                                   gt[par][:, hb * TSC:(hb + 1) * TSC])
            self.h_p4bv[par] = C["v"].inc(ins)
            t.wait_ge(C["v"].h, self.h_p4bv[par])
            for tq in range(TSC // 128):
                q = (ts * TSC) // 128 + tq
                app = q % 2
                ps = R["pa"][2 + app]
                t.wait_ge(C["s"].h, self.h_ev[app])
                ins = None
                for dh in range(2):
                    for hb in range(KH):
                        ins = t.matmul(
                            ps[:, dh * 512:(dh + 1) * 512],
                            ya[par][:, hb * TSC + tq * 128:
                                    hb * TSC + tq * 128 + 128],
                            w_out[:, hb * D + dh * 512:
                                  hb * D + dh * 512 + 512],
                            start=(hb == 0), stop=(hb == KH - 1))
                self.h_p4bmm[par] = C["t"].inc(ins)
                s.wait_ge(C["t"].h, C["t"].v)
                s.wait_ge(C["d_acc"].h, self.h_acc[app])
                ev = R["EV"][:, app * D:(app + 1) * D]
                ins = s.copy(out=ev, in_=ps[:])
                self.h_ev[app] = C["s"].inc(ins)
                g.wait_ge(C["s"].h, self.h_ev[app])
                ins = g.dma_start(out=dst[q * 128:(q + 1) * 128, :], in_=ev,
                                  accum_op=ALU.add)
                self.h_acc[app] = C["d_acc"].inc(ins, 16)

def kernel(L0_attn_x, L0_attn_norm_w, L0_attn_norm_b, L0_attn_W_in,
           L0_attn_W_gate, L0_attn_W_dt, L0_attn_b_dt, L0_attn_W_B,
           L0_attn_W_C, L0_attn_A_log, L0_attn_W_out, L0_ffn_shared_gate,
           L0_ffn_shared_up, L0_ffn_shared_down, L0_ffn_routed_gate,
           L0_ffn_routed_up, L0_ffn_routed_down, L0_ffn_Wr,
           L0_ffn_routing_bias, _stop=None):
    nc = build_nc(stop=_stop)
    xs = np.ascontiguousarray(np.asarray(L0_attn_x, dtype=np.float32))
    common = dict(
        norm_w=np.ascontiguousarray(L0_attn_norm_w, dtype=np.float32),
        norm_b=np.ascontiguousarray(L0_attn_norm_b, dtype=np.float32),
        W_in=np.ascontiguousarray(L0_attn_W_in, dtype=np.float32),
        W_gate=np.ascontiguousarray(L0_attn_W_gate, dtype=np.float32),
        W_dt=np.ascontiguousarray(L0_attn_W_dt, dtype=np.float32),
        b_dt=np.ascontiguousarray(L0_attn_b_dt, dtype=np.float32),
        W_B=np.ascontiguousarray(L0_attn_W_B, dtype=np.float32),
        W_C=np.ascontiguousarray(L0_attn_W_C, dtype=np.float32),
        A_log=np.ascontiguousarray(L0_attn_A_log, dtype=np.float32),
        W_out=np.ascontiguousarray(L0_attn_W_out, dtype=np.float32),
        sh_gate=np.ascontiguousarray(L0_ffn_shared_gate, dtype=np.float32),
        sh_up=np.ascontiguousarray(L0_ffn_shared_up, dtype=np.float32),
        sh_down=np.ascontiguousarray(L0_ffn_shared_down, dtype=np.float32),
        r_gate=np.ascontiguousarray(L0_ffn_routed_gate, dtype=np.float32),
        r_up=np.ascontiguousarray(L0_ffn_routed_up, dtype=np.float32),
        r_down=np.ascontiguousarray(L0_ffn_routed_down, dtype=np.float32),
        Wr=np.ascontiguousarray(L0_ffn_Wr, dtype=np.float32),
        r_bias=np.ascontiguousarray(L0_ffn_routing_bias, dtype=np.float32),
    )
    in_maps = [dict(common, x0=np.ascontiguousarray(xs[c])) for c in range(8)]
    res = run_bass_kernel_spmd(nc, in_maps, core_ids=list(range(8)))
    return np.stack([r["out"] for r in res.results], axis=0)


# version 08
# speedup vs baseline: 1.0093x; 1.0093x over previous
import numpy as np

import concourse.bass as bass
import concourse.mybir as mybir
from concourse.bass_utils import run_bass_kernel_spmd

F32 = mybir.dt.float32
BF16 = mybir.dt.bfloat16
AF = mybir.ActivationFunctionType
ALU = mybir.AluOpType
AXX = mybir.AxisListType.X

T, D, HID, DI, NS = 8192, 1024, 2816, 2048, 16
KD = D // 128
KH = DI // 128
NQ = T // 128
TBF = 512
NTBF = T // TBF
TSC = 256
NTS = T // TSC
TB3 = 512
NTB3 = T // TB3
NHT = HID // 128
CHUNKS = [(0, 6), (6, 12), (12, 18), (18, 22)]
CHMAX = 6
EPS = 1e-5
ENGINES = ("sync", "gpsimd", "vector", "scalar", "tensor")


def _noop(*a, **k):
    return None


class Shim:
    __slots__ = ("r",)

    def __init__(self, real):
        self.r = real

    def __getattr__(self, n):
        if self.r is None:
            return _noop
        return getattr(self.r, n)


class CSem:
    def __init__(self, h):
        self.h = h
        self.v = 0

    def inc(self, ins, n=1):
        self.v += n
        if ins is not None:
            ins.then_inc(self.h, n)
        return self.v


def build_nc(stop=None):
    nc = bass.Bass("TRN2", target_bir_lowering=False)

    x0 = nc.dram_tensor("x0", [T, D], F32, kind="ExternalInput").ap()
    wi = {}
    for name, shape in [
        ("norm_w", [D]), ("norm_b", [D]), ("W_in", [D, DI]), ("W_gate", [D, DI]),
        ("W_dt", [DI, DI]), ("b_dt", [DI]), ("W_B", [DI, NS]), ("W_C", [DI, NS]),
        ("A_log", [DI, NS]), ("W_out", [DI, D]),
        ("sh_gate", [D, HID]), ("sh_up", [D, HID]), ("sh_down", [HID, D]),
        ("r_gate", [7, D, HID]), ("r_up", [7, D, HID]), ("r_down", [7, HID, D]),
        ("Wr", [D, 7]), ("r_bias", [7]),
    ]:
        wi[name] = nc.dram_tensor(name, shape, F32, kind="ExternalInput").ap()
    out = nc.dram_tensor("out", [T, D], F32, kind="ExternalOutput").ap()

    x1 = nc.dram_tensor("x1", [T, D], F32, kind="Internal").ap()
    x2 = nc.dram_tensor("x2", [T, D], F32, kind="Internal").ap()
    xnbT = nc.dram_tensor("xnbT", [D, T], BF16, kind="Internal").ap()
    zth = nc.dram_tensor("zth", [DI, T], BF16, kind="Internal").ap()
    dth = nc.dram_tensor("dth", [DI, T], BF16, kind="Internal").ap()
    dtzh = nc.dram_tensor("dtzh", [DI, T], BF16, kind="Internal").ap()
    gateh = nc.dram_tensor("gateh", [DI, T], BF16, kind="Internal").ap()
    bch = nc.dram_tensor("bch", [NS, 2, T], BF16, kind="Internal").ap()

    from contextlib import ExitStack
    ctx = ExitStack()

    def sb(name, shape, dtype):
        return ctx.enter_context(nc.sbuf_tensor(name, shape, dtype))

    R = dict(x0=x0, wi=wi, out=out, x1=x1, x2=x2, xnbT=xnbT, zth=zth, dth=dth,
             dtzh=dtzh, gateh=gateh, bch=bch)
    R["POOLW"] = sb("POOLW", [128, 38912], BF16)
    R["XNT"] = sb("XNT", [128, 2 * KD * TBF], BF16)
    R["HT"] = sb("HT", [128, CHMAX * TBF], BF16)
    R["HSIL"] = sb("HSIL", [128, 2 * TBF], BF16)
    R["EV"] = sb("EV", [128, 2 * D], F32)
    R["STG"] = sb("STG", [128, 2 * D], BF16)
    R["WSB"] = sb("WSB", [128, NQ * 8], F32)
    R["WRSB"] = sb("WRSB", [128, KD * 8], F32)
    R["BIASR"] = sb("BIASR", [128, 8], F32)
    R["IDF"] = sb("IDF", [128, 128], F32)
    R["IDB"] = sb("IDB", [128, 128], BF16)
    R["XTI"] = sb("XTI", [128, 2 * D], F32)
    R["XN32"] = sb("XN32", [128, D], F32)
    R["XNT32"] = sb("XNT32", [128, D], F32)
    R["DUM"] = sb("DUM", [128, D], F32)
    R["STT"] = sb("STT", [128, 24], F32)
    R["EPSC"] = sb("EPSC", [128, 1], F32)
    R["POOLM"] = sb("POOLM", [128, 28672], BF16)
    R["ASC"] = sb("ASC", [128, KH * NS], F32)
    R["BDT"] = sb("BDT", [128, KH], F32)
    R["WBC"] = sb("WBC", [128, 2 * KH * NS], BF16)
    R["HST"] = sb("HST", [128, KH * NS], F32)
    R["pa"] = [ctx.enter_context(nc.psum_tensor(f"pa{i}", [128, 1024], F32))
               for i in range(4)]

    sems = {}
    for nm in ["t", "v", "s", "g", "d_w0", "d_w1", "d_a0", "d_a1",
               "d_b0", "d_b1", "d_st0", "d_st1", "d_acc", "d_bc0", "d_bc1",
               "d_cp"]:
        sems[nm] = ctx.enter_context(nc.semaphore(nm))

    b = Builder(nc, R, sems, stop)
    with nc.Block() as block:
        block.sync(lambda e: b.emit("sync", e))
        block.gpsimd(lambda e: b.emit("gpsimd", e))
        block.vector(lambda e: b.emit("vector", e))
        block.scalar(lambda e: b.emit("scalar", e))
        block.tensor(lambda e: b.emit("tensor", e))
    ctx.close()
    return nc


class Builder:
    def __init__(self, nc, R, sems, stop):
        self.nc = nc
        self.R = R
        self.sems = sems
        self.stop = stop

    def emit(self, which, eng):
        self.which = which
        E = {nm: Shim(eng if nm == which else None) for nm in ENGINES}
        self.t, self.v, self.s = E["tensor"], E["vector"], E["scalar"]
        self.sy, self.g = E["sync"], E["gpsimd"]
        self.E = E
        self.C = {nm: CSem(h) for nm, h in self.sems.items()}
        self.h_silu = [0, 0]; self.h_hmul = [0, 0]
        self.h_ev = [0, 0]; self.h_acc = [0, 0]
        self.h_xnt = [0, 0]; self.h_wchunk = [0, 0]
        self.h_p1ev = 0; self.h_p1v = 0; self.h_stg = [0, 0]
        self.h_x2st = [0, 0]; self.h_trev = 0; self.h_zev = [0, 0]
        self.h_gst = [0, 0]; self.h_zst = 0; self.h_p3v = 0
        self.h_dtv = [0, 0]; self.h_dtst = [0, 0]; self.h_bcev = 0
        self.h_scan = [0, 0, 0, 0]; self.h_dA = [0, 0, 0, 0]; self.h_ygst = 0
        self.h_sdone = [0, 0]; self.h_p4ld = 0; self.h_p4bmm = [0, 0]
        self.h_routerv = 0; self.h_xti = [0, 0]; self.h_xtis = [0, 0]
        self.h_trt = 0; self._wtarget = {}
        self.h_dtuse_s = [0, 0]; self.h_dtuse_v = [0, 0]
        self.h_accq = {}
        self.h_p4bv = [0, 0]

        R = self.R
        self.prologue()
        self.barrier()
        dst1 = R["out"] if self.stop == "ffn1" else R["x1"]
        self.copy_resid(src=R["x0"], dst=dst1)
        self.drain(self.p_ln_router(src=R["x0"], q_lo=0, q_hi=NQ))
        self.barrier()
        if self.stop == "ffn1":
            self.drain(self.p_experts(dst=dst1, q_lo=0, q_hi=NQ))
            self.barrier()
            self.epilogue()
            return
        if self.stop == "mamba":
            self.drain(self.p_experts(dst=dst1, q_lo=0, q_hi=NQ))
            self.barrier()
            self.p3a(src=dst1, tb_lo=0, tb_hi=NTB3)
            self.barrier()
            self.p3b(tb_lo=0, tb_hi=NTB3)
            self.barrier()
            self.drain(self.p4a(ts_lo=0, ts_hi=NTS))
            self.barrier()
            self.copy_resid(src=R["x2"], dst=R["out"])
            self.barrier()
            self.drain(self.p4b(dst=R["out"], ts_lo=0, ts_hi=NTS))
            self.barrier()
            self.epilogue()
            return
        HQ, HTB, HTS = NQ // 2, NTB3 // 2, NTS // 2
        self.drain(self.p_experts(dst=dst1, q_lo=0, q_hi=HQ))
        self.barrier()
        self.p3a(src=dst1, tb_lo=0, tb_hi=HTB)
        self.barrier()
        self.p3b(tb_lo=0, tb_hi=HTB)
        self.barrier()
        self.interleave(self.p_experts(dst=dst1, q_lo=HQ, q_hi=NQ),
                        self.p4a(ts_lo=0, ts_hi=HTS))
        self.barrier()
        self.p3a(src=dst1, tb_lo=HTB, tb_hi=NTB3)
        self.barrier()
        self.p3b(tb_lo=HTB, tb_hi=NTB3)
        self.barrier()
        self.interleave(self.p4b(dst=R["x2"], ts_lo=0, ts_hi=HTS),
                        self.p_ln_router(src=R["x2"], q_lo=0, q_hi=HQ,
                                         gate=True), nb=2)
        self.barrier()
        self.copy_resid(src=R["x2"], dst=R["out"], r_lo=0, r_hi=HQ)
        self.barrier()
        self.interleave(self.p_experts(dst=R["out"], q_lo=0, q_hi=HQ),
                        self.p4a(ts_lo=HTS, ts_hi=NTS))
        self.barrier()
        self.interleave(self.p4b(dst=R["x2"], ts_lo=HTS, ts_hi=NTS),
                        self.p_ln_router(src=R["x2"], q_lo=HQ, q_hi=NQ,
                                         gate=True), nb=2)
        self.barrier()
        self.copy_resid(src=R["x2"], dst=R["out"], r_lo=HQ, r_hi=NQ)
        self.barrier()
        self.drain(self.p_experts(dst=R["out"], q_lo=HQ, q_hi=NQ))
        self.barrier()
        self.epilogue()

    def drain(self, gen):
        for _ in gen:
            pass

    def interleave(self, ga, gb, nb=1):
        alive_a = alive_b = True
        while alive_a or alive_b:
            if alive_a:
                try:
                    next(ga)
                except StopIteration:
                    alive_a = False
            for _ in range(nb):
                if alive_b:
                    try:
                        next(gb)
                    except StopIteration:
                        alive_b = False

    def vf(self, ins):
        C = self.C
        C["v"].inc(ins)
        self.v.wait_ge(C["v"].h, C["v"].v)

    def sf(self, ins):
        C = self.C
        C["s"].inc(ins)
        self.s.wait_ge(C["s"].h, C["s"].v)

    def barrier(self):
        snap = {"t": self.C["t"].v, "v": self.C["v"].v,
                "s": self.C["s"].v, "g": self.C["g"].v}
        own = {"tensor": "t", "vector": "v", "scalar": "s", "gpsimd": "g"}
        for enm in ENGINES:
            e = self.E[enm]
            for om, val in snap.items():
                if val > 0 and om != own.get(enm):
                    e.wait_ge(self.C[om].h, val)

    def epilogue(self):
        C = self.C
        self.g.wait_ge(C["d_acc"].h, C["d_acc"].v)
        for nm in ("d_st0", "d_st1", "d_cp"):
            self.sy.wait_ge(C[nm].h, C[nm].v)
        self.barrier()

    def prologue(self):
        R, C = self.R, self.C
        t, v, s, sy, g = self.t, self.v, self.s, self.sy, self.g
        wi = R["wi"]
        g.memset(R["IDF"][:], 0.0)
        g.affine_select(out=R["IDF"][:], in_=R["IDF"][:],
                        compare_op=ALU.not_equal, fill=1.0, base=0,
                        pattern=[[-1, 128]], channel_multiplier=1)
        g.memset(R["IDB"][:], 0.0)
        g.affine_select(out=R["IDB"][:], in_=R["IDB"][:],
                        compare_op=ALU.not_equal, fill=1.0, base=0,
                        pattern=[[-1, 128]], channel_multiplier=1)
        g.memset(R["WRSB"][:], 0.0)
        g.memset(R["EPSC"][:], EPS)
        ins = g.memset(R["BIASR"][:], 0.0)
        C["g"].inc(ins)
        g.wait_ge(C["g"].h, C["g"].v)
        ins = g.dma_start(
            out=R["WRSB"][:].rearrange("p (kb e) -> p kb e", e=8)[:, :, 0:7],
            in_=wi["Wr"].rearrange("(kb p) e -> p kb e", p=128))
        C["d_w0"].inc(ins, 16)
        ins = g.dma_start(out=R["BIASR"][:, 0:7],
                          in_=wi["r_bias"].unsqueeze(0)
                          .partition_broadcast(128).squeeze(1))
        C["d_w0"].inc(ins, 16)
        ins = g.dma_start(
            out=R["ASC"][:].rearrange("p (hb s) -> p hb s", s=NS),
            in_=wi["A_log"].rearrange("(hb p) s -> p hb s", p=128))
        C["d_w0"].inc(ins, 16)
        ins = g.dma_start(out=R["BDT"][:],
                          in_=wi["b_dt"].rearrange("(hb p) -> p hb", p=128),
                          allow_slow_non_contiguous=True)
        C["d_w0"].inc(ins, 16)
        ins = g.dma_start(
            out=R["WBC"][:, 0:KH * NS].rearrange("p (kb s) -> p kb s", s=NS),
            in_=wi["W_B"].rearrange("(kb p) s -> p kb s", p=128))
        C["d_w0"].inc(ins, 16)
        ins = g.dma_start(
            out=R["WBC"][:, KH * NS:2 * KH * NS]
            .rearrange("p (kb s) -> p kb s", s=NS),
            in_=wi["W_C"].rearrange("(kb p) s -> p kb s", p=128))
        C["d_w0"].inc(ins, 16)
        s.wait_ge(C["d_w0"].h, C["d_w0"].v)
        ins = s.activation(out=R["ASC"][:], in_=R["ASC"][:], func=AF.Exp)
        C["s"].inc(ins)
        v.wait_ge(C["s"].h, C["s"].v)
        v.tensor_scalar(out=R["ASC"][:], in0=R["ASC"][:], scalar1=1e-6,
                        scalar2=1.0, op0=ALU.max, op1=ALU.min)
        ins = v.tensor_scalar(out=R["ASC"][:], in0=R["ASC"][:], scalar1=-1.0,
                              scalar2=None, op0=ALU.mult)
        C["v"].inc(ins)
        ins = v.memset(R["HST"][:], 0.0)
        C["v"].inc(ins)
        for enm in ENGINES:
            self.E[enm].wait_ge(C["v"].h, C["v"].v)
            self.E[enm].wait_ge(C["g"].h, C["g"].v)
            self.E[enm].wait_ge(C["d_w0"].h, C["d_w0"].v)

    def dma_gate(self, *sems):
        C = self.C
        for nm in sems:
            self.sy.wait_ge(C[nm].h, C[nm].v)
            self.g.wait_ge(C[nm].h, C[nm].v)

    def copy_resid(self, src, dst, r_lo=0, r_hi=NQ):
        C = self.C
        self.dma_gate("d_acc", "d_st0", "d_st1")
        ins = self.sy.dma_start(out=dst[r_lo * 128:r_hi * 128, :],
                                in_=src[r_lo * 128:r_hi * 128, :])
        C["d_cp"].inc(ins, 16)
        self.g.wait_ge(C["d_cp"].h, C["d_cp"].v)
        self.sy.wait_ge(C["d_cp"].h, C["d_cp"].v)

    def _ln_tile(self, xt, dst32):
        R, C = self.R, self.C
        s, v = self.s, self.v
        STr = R["STT"]
        s.activation(out=R["DUM"][:], in_=xt, func=AF.Copy,
                     accum_out=STr[:, 0:1])
        ins = s.activation(out=R["DUM"][:], in_=xt, func=AF.Square,
                           accum_out=STr[:, 1:2])
        self.sf(ins)
        s.mul(out=STr[:, 2:3], in_=STr[:, 0:1], mul=1.0 / D)
        ins = s.mul(out=STr[:, 3:4], in_=STr[:, 1:2], mul=1.0 / D)
        self.sf(ins)
        ins = s.activation(out=STr[:, 4:5], in_=STr[:, 2:3], func=AF.Square)
        C["s"].inc(ins)
        v.wait_ge(C["s"].h, C["s"].v)
        ins = v.tensor_sub(STr[:, 5:6], STr[:, 3:4], STr[:, 4:5])
        C["v"].inc(ins)
        s.wait_ge(C["v"].h, C["v"].v)
        ins = s.activation(out=STr[:, 6:7], in_=STr[:, 5:6], func=AF.Ln,
                           bias=R["EPSC"][:, 0:1])
        self.sf(ins)
        ins = s.activation(out=STr[:, 7:8], in_=STr[:, 6:7], func=AF.Exp,
                           scale=-0.5)
        C["s"].inc(ins)
        v.wait_ge(C["s"].h, C["s"].v)
        ins = v.tensor_scalar(out=dst32, in0=xt, scalar1=STr[:, 2:3],
                              scalar2=STr[:, 7:8], op0=ALU.subtract,
                              op1=ALU.mult)
        return C["v"].inc(ins)

    def p_ln_router(self, src, q_lo, q_hi, gate=False):
        R, C = self.R, self.C
        t, v, s, sy, g = self.t, self.v, self.s, self.sy, self.g
        if not gate:
            self.dma_gate("d_acc")
        dpar = [C["d_a0"], C["d_a1"]]
        spar = [C["d_st0"], C["d_st1"]]
        STr = R["STT"]
        for q in range(q_lo, q_hi):
            par = q % 2
            xt = R["XTI"][:, par * D:(par + 1) * D]
            if gate and q in self.h_accq:
                sy.wait_ge(C["d_acc"].h, self.h_accq[q])
            sy.wait_ge(C["v"].h, self.h_xti[par])
            sy.wait_ge(C["s"].h, self.h_xtis[par])
            ins = sy.dma_start(out=xt, in_=src[q * 128:(q + 1) * 128, :])
            wv = dpar[par].inc(ins, 16)
            s.wait_ge(C["v"].h, self.h_p1v)
            s.wait_ge(dpar[par].h, wv)
            v.wait_ge(C["t"].h, self.h_trt)
            wv_xn = self._ln_tile(xt, R["XN32"][:])
            self.h_xtis[par] = C["s"].v
            t.wait_ge(C["v"].h, wv_xn)
            t.wait_ge(C["s"].h, self.h_p1ev)
            ins = None
            for k in range(KD):
                ins = t.transpose(R["pa"][0][:, k * 128:(k + 1) * 128],
                                  R["XN32"][:, k * 128:(k + 1) * 128],
                                  R["IDF"][:])
            C["t"].inc(ins)
            self.h_trt = C["t"].v
            s.wait_ge(C["t"].h, C["t"].v)
            s.wait_ge(spar[par].h, self.h_stg[par])
            stg = R["STG"][:, par * D:(par + 1) * D]
            s.copy(out=R["XNT32"][:], in_=R["pa"][0][:])
            ins = s.copy(out=stg, in_=R["pa"][0][:])
            self.h_p1ev = C["s"].inc(ins)
            sy.wait_ge(C["s"].h, self.h_p1ev)
            ins = sy.dma_start(
                out=R["xnbT"].rearrange("(kb p) t -> p kb t", p=128)
                [:, :, q * 128:(q + 1) * 128],
                in_=stg.rearrange("p (kb t) -> p kb t", t=128))
            self.h_stg[par] = spar[par].inc(ins, 16)
            t.wait_ge(C["v"].h, self.h_routerv)
            t.wait_ge(C["s"].h, self.h_p1ev)
            ins = None
            for kb in range(KD):
                ins = t.matmul(R["pa"][1][:, 0:8],
                               R["XNT32"][:, kb * 128:(kb + 1) * 128],
                               R["WRSB"][:, kb * 8:(kb + 1) * 8],
                               start=(kb == 0), stop=(kb == KD - 1))
            C["t"].inc(ins)
            v.wait_ge(C["t"].h, C["t"].v)
            PL = STr[:, 8:16]
            ins = v.tensor_mul(PL, R["pa"][1][:, 0:8], R["BIASR"][:])
            self.h_routerv = C["v"].inc(ins)
            s.wait_ge(C["v"].h, self.h_routerv)
            ins = s.activation(out=PL, in_=PL, func=AF.Sigmoid)
            C["s"].inc(ins)
            v.wait_ge(C["s"].h, C["s"].v)
            MK = R["DUM"][:, 0:8]
            P2 = R["DUM"][:, 8:16]
            HUGE = 1.0e30
            self.vf(v.memset(STr[:, 15:16], -1.0))
            self.vf(v.tensor_reduce(STr[:, 16:17], PL, axis=AXX, op=ALU.max))
            self.vf(v.tensor_scalar(out=MK, in0=PL, scalar1=STr[:, 16:17],
                                    scalar2=-HUGE, op0=ALU.subtract,
                                    op1=ALU.mult))
            self.vf(v.tensor_scalar(out=MK, in0=MK, scalar1=1.0, scalar2=0.0,
                                    op0=ALU.min, op1=ALU.max))
            self.vf(v.tensor_mul(P2, PL, MK))
            self.vf(v.tensor_reduce(STr[:, 17:18], P2, axis=AXX, op=ALU.max))
            self.vf(v.tensor_add(STr[:, 18:19], STr[:, 16:17],
                                 STr[:, 17:18]))
            self.vf(v.reciprocal(STr[:, 19:20], STr[:, 18:19]))
            self.vf(v.tensor_scalar(out=MK, in0=PL, scalar1=STr[:, 17:18],
                                    scalar2=-HUGE, op0=ALU.subtract,
                                    op1=ALU.mult))
            self.vf(v.tensor_scalar(out=MK, in0=MK, scalar1=1.0, scalar2=0.0,
                                    op0=ALU.min, op1=ALU.max))
            self.vf(v.tensor_mul(MK, PL, MK))
            self.vf(v.tensor_sub(P2, PL, MK))
            ins = v.tensor_scalar(out=R["WSB"][:, q * 8:(q + 1) * 8],
                                  in0=P2, scalar1=STr[:, 19:20], scalar2=0.5,
                                  op0=ALU.mult, op1=ALU.mult)
            self.h_p1v = C["v"].inc(ins)
            self.h_xti[par] = self.h_p1v
            yield

    def _wviews(self, par, n):
        base = par * 18432
        W = self.R["POOLW"]
        wg = W[:, base:base + KD * n * 128]
        wu = W[:, base + 6144:base + 6144 + KD * n * 128]
        wd = W[:, base + 12288:base + 12288 + n * D]
        return wg, wu, wd

    def p_experts(self, dst, q_lo, q_hi):
        R, C = self.R, self.C
        t, v, s, sy, g = self.t, self.v, self.s, self.sy, self.g
        self.dma_gate("d_st0", "d_st1", "d_cp")
        wi = R["wi"]
        tb_lo, tb_hi = q_lo // 4, q_hi // 4
        wpar = [C["d_w0"], C["d_w1"]]
        xpar = [C["d_b0"], C["d_b1"]]
        steps = [(e, ci) for e in range(8) for ci in range(len(CHUNKS))]

        def issue_chunk(i):
            if i >= len(steps):
                return
            e, ci = steps[i]
            par = i % 2
            lo, hi = CHUNKS[ci]
            n = hi - lo
            g.wait_ge(C["t"].h, self.h_wchunk[par])
            wg, wu, wd = self._wviews(par, n)
            if e == 0:
                gs, us, ds = wi["sh_gate"], wi["sh_up"], wi["sh_down"]
            else:
                gs, us, ds = (wi["r_gate"][e - 1], wi["r_up"][e - 1],
                              wi["r_down"][e - 1])
            ins = g.dma_start(
                out=wg.rearrange("p (kb h) -> p kb h", kb=KD),
                in_=gs.rearrange("(kb p) h -> p kb h", p=128)
                [:, :, lo * 128:hi * 128])
            wpar[par].inc(ins, 16)
            ins = g.dma_start(
                out=wu.rearrange("p (kb h) -> p kb h", kb=KD),
                in_=us.rearrange("(kb p) h -> p kb h", p=128)
                [:, :, lo * 128:hi * 128])
            wpar[par].inc(ins, 16)
            ins = g.dma_start(
                out=wd.rearrange("p (hb d) -> p hb d", hb=n),
                in_=ds.rearrange("(hb p) d -> p hb d", p=128)[:, lo:hi, :])
            self._wtarget[i] = wpar[par].inc(ins, 16)

        issue_chunk(0)
        issue_chunk(1)

        for i, (e, ci) in enumerate(steps):
            par = i % 2
            lo, hi = CHUNKS[ci]
            n = hi - lo
            wg, wu, wd = self._wviews(par, n)
            t.wait_ge(wpar[par].h, self._wtarget[i])
            for tb in range(tb_lo, tb_hi):
                xp = tb % 2
                xnt = R["XNT"][:, xp * KD * TBF:(xp + 1) * KD * TBF]
                sy.wait_ge(C["t"].h, self.h_xnt[xp])
                ins = sy.dma_start(
                    out=xnt.rearrange("p (kb t) -> p kb t", kb=KD),
                    in_=R["xnbT"].rearrange("(kb p) t -> p kb t", p=128)
                    [:, :, tb * TBF:(tb + 1) * TBF])
                wv_x = xpar[xp].inc(ins, 16)
                t.wait_ge(xpar[xp].h, wv_x)
                for j in range(n):
                    pp = j % 2
                    psg = R["pa"][0][:, pp * 512:(pp + 1) * 512]
                    psu = R["pa"][1][:, pp * 512:(pp + 1) * 512]
                    t.wait_ge(C["v"].h, self.h_hmul[pp])
                    ins = None
                    for kb in range(KD):
                        ins = t.matmul(psg,
                                       wg[:, (kb * n + j) * 128:
                                          (kb * n + j + 1) * 128],
                                       xnt[:, kb * TBF:(kb + 1) * TBF],
                                       start=(kb == 0), stop=(kb == KD - 1))
                    C["t"].inc(ins)
                    for kb in range(KD):
                        ins = t.matmul(psu,
                                       wu[:, (kb * n + j) * 128:
                                          (kb * n + j + 1) * 128],
                                       xnt[:, kb * TBF:(kb + 1) * TBF],
                                       start=(kb == 0), stop=(kb == KD - 1))
                    C["t"].inc(ins)
                    s.wait_ge(C["t"].h, C["t"].v)
                    hs = R["HSIL"][:, pp * TBF:(pp + 1) * TBF]
                    ins = s.activation(out=hs, in_=psg, func=AF.Silu)
                    self.h_silu[pp] = C["s"].inc(ins)
                    v.wait_ge(C["s"].h, self.h_silu[pp])
                    ins = v.tensor_mul(R["HT"][:, j * TBF:(j + 1) * TBF],
                                       hs, psu)
                    self.h_hmul[pp] = C["v"].inc(ins)
                self.h_xnt[xp] = C["t"].v
                yield
                t.wait_ge(C["v"].h, self.h_hmul[0])
                t.wait_ge(C["v"].h, self.h_hmul[1])
                for tq in range(4):
                    q = tb * 4 + tq
                    app = q % 2
                    ps = R["pa"][2 + app]
                    t.wait_ge(C["s"].h, self.h_ev[app])
                    ins = None
                    for dh in range(2):
                        for j in range(n):
                            ins = t.matmul(
                                ps[:, dh * 512:(dh + 1) * 512],
                                R["HT"][:, j * TBF + tq * 128:
                                        j * TBF + tq * 128 + 128],
                                wd[:, j * D + dh * 512:j * D + dh * 512 + 512],
                                start=(j == 0), stop=(j == n - 1))
                    C["t"].inc(ins)
                    s.wait_ge(C["t"].h, C["t"].v)
                    s.wait_ge(C["d_acc"].h, self.h_acc[app])
                    ev = R["EV"][:, app * D:(app + 1) * D]
                    if e == 0:
                        ins = s.activation(out=ev, in_=ps[:], func=AF.Copy,
                                           scale=0.5)
                    else:
                        ins = s.activation(out=ev, in_=ps[:], func=AF.Copy,
                                           scale=R["WSB"][:, q * 8 + e - 1:
                                                          q * 8 + e])
                    self.h_ev[app] = C["s"].inc(ins)
                    g.wait_ge(C["s"].h, self.h_ev[app])
                    ins = g.dma_start(out=dst[q * 128:(q + 1) * 128, :],
                                      in_=ev, accum_op=ALU.add)
                    self.h_acc[app] = C["d_acc"].inc(ins, 16)
                yield
            self.h_wchunk[par] = C["t"].v
            issue_chunk(i + 2)

    def p3a(self, src, tb_lo, tb_hi):
        R, C = self.R, self.C
        t, v, s, sy, g = self.t, self.v, self.s, self.sy, self.g
        self.dma_gate("d_acc")
        wi = R["wi"]
        W = R["POOLW"]
        w_in = W[:, 0:KD * DI]
        w_gt = W[:, KD * DI:2 * KD * DI]
        ins = g.dma_start(out=w_in.rearrange("p (kb h) -> p kb h", kb=KD),
                          in_=wi["W_in"].rearrange("(kb p) h -> p kb h",
                                                   p=128))
        C["d_w0"].inc(ins, 16)
        ins = g.dma_start(out=w_gt.rearrange("p (kb h) -> p kb h", kb=KD),
                          in_=wi["W_gate"].rearrange("(kb p) h -> p kb h",
                                                     p=128))
        C["d_w0"].inc(ins, 16)
        M = R["POOLM"]
        nwr = M[:, 0:2048].bitcast(F32)
        nbr = M[:, 2048:4096].bitcast(F32)
        ins = g.dma_start(out=nwr, in_=wi["norm_w"].unsqueeze(0)
                          .partition_broadcast(128).squeeze(1))
        C["d_bc0"].inc(ins, 16)
        ins = g.dma_start(out=nbr, in_=wi["norm_b"].unsqueeze(0)
                          .partition_broadcast(128).squeeze(1))
        C["d_bc0"].inc(ins, 16)
        v.wait_ge(C["d_bc0"].h, C["d_bc0"].v)
        t.wait_ge(C["d_w0"].h, C["d_w0"].v)

        dpar = [C["d_a0"], C["d_a1"]]
        zt_stage = M[:, 4096:4096 + KH * TB3]
        xnm_bf = M[:, 4096 + KH * TB3:4096 + KH * TB3 + D]
        wv_mm = 0
        for tb in range(tb_lo, tb_hi):
            for qq in range(4):
                q = tb * 4 + qq
                par = q % 2
                xt = R["XTI"][:, par * D:(par + 1) * D]
                sy.wait_ge(C["v"].h, self.h_xti[par])
                sy.wait_ge(C["s"].h, self.h_xtis[par])
                ins = sy.dma_start(out=xt, in_=src[q * 128:(q + 1) * 128, :])
                wv = dpar[par].inc(ins, 16)
                s.wait_ge(C["v"].h, self.h_p3v)
                s.wait_ge(dpar[par].h, wv)
                v.wait_ge(C["t"].h, self.h_trt)
                self._ln_tile(xt, R["XN32"][:])
                v.wait_ge(C["d_st0"].h, self.h_x2st[par])
                ev = R["EV"][:, par * D:(par + 1) * D]
                ins = v.tensor_add(ev, xt, R["XN32"][:])
                wv2 = C["v"].inc(ins)
                sy.wait_ge(C["v"].h, wv2)
                ins = sy.dma_start(out=R["x2"][q * 128:(q + 1) * 128, :],
                                   in_=ev)
                self.h_x2st[par] = C["d_st0"].inc(ins, 16)
                s.wait_ge(C["v"].h, wv2)
                self._ln_tile(R["XN32"][:], R["XN32"][:])
                self.h_xtis[par] = C["s"].v
                ins = v.tensor_mul(R["XN32"][:], R["XN32"][:], nwr)
                ins = v.tensor_add(xnm_bf, R["XN32"][:], nbr)
                wv3 = C["v"].inc(ins)
                self.h_p3v = wv3
                self.h_xti[par] = wv2
                t.wait_ge(C["v"].h, wv3)
                t.wait_ge(C["s"].h, self.h_trev)
                pab = R["pa"][3][:].bitcast(BF16)
                ins = None
                for k in range(KD):
                    ins = t.transpose(pab[:, k * 128:(k + 1) * 128],
                                      xnm_bf[:, k * 128:(k + 1) * 128],
                                      R["IDB"][:])
                C["t"].inc(ins)
                self.h_trt = C["t"].v
                s.wait_ge(C["t"].h, C["t"].v)
                if qq == 0:
                    s.wait_ge(C["t"].h, wv_mm)
                xnt = R["XNT"][:, 0:KD * TB3]
                ins = s.copy(
                    out=xnt.rearrange("p (kb t) -> p kb t", t=TB3)
                    [:, :, qq * 128:(qq + 1) * 128],
                    in_=pab[:, 0:KD * 128].rearrange("p (kb t) -> p kb t",
                                                     t=128))
                self.h_trev = C["s"].inc(ins)
            t.wait_ge(C["s"].h, self.h_trev)
            for hb in range(KH):
                pp = hb % 2
                psz = R["pa"][0][:, pp * 512:(pp + 1) * 512]
                psgt = R["pa"][1][:, pp * 512:(pp + 1) * 512]
                t.wait_ge(C["s"].h, self.h_zev[pp])
                ins = None
                for kb in range(KD):
                    ins = t.matmul(psz,
                                   w_in[:, kb * DI + hb * 128:
                                        kb * DI + (hb + 1) * 128],
                                   xnt[:, kb * TB3:(kb + 1) * TB3],
                                   start=(kb == 0), stop=(kb == KD - 1))
                C["t"].inc(ins)
                for kb in range(KD):
                    ins = t.matmul(psgt,
                                   w_gt[:, kb * DI + hb * 128:
                                        kb * DI + (hb + 1) * 128],
                                   xnt[:, kb * TB3:(kb + 1) * TB3],
                                   start=(kb == 0), stop=(kb == KD - 1))
                C["t"].inc(ins)
                wv_mm = C["t"].v
                s.wait_ge(C["t"].h, C["t"].v)
                if hb == 0:
                    s.wait_ge(C["d_st0"].h, self.h_zst)
                s.wait_ge(C["d_st1"].h, self.h_gst[pp])
                s.copy(out=zt_stage[:, hb * TB3:(hb + 1) * TB3], in_=psz)
                gstage = R["HSIL"][:, pp * TB3:(pp + 1) * TB3]
                ins = s.activation(out=gstage, in_=psgt, func=AF.Silu)
                self.h_zev[pp] = C["s"].inc(ins)
                sy.wait_ge(C["s"].h, self.h_zev[pp])
                ins = sy.dma_start(
                    out=R["gateh"][hb * 128:(hb + 1) * 128,
                                   tb * TB3:(tb + 1) * TB3], in_=gstage)
                self.h_gst[pp] = C["d_st1"].inc(ins, 16)
            sy.wait_ge(C["s"].h, self.h_zev[0])
            sy.wait_ge(C["s"].h, self.h_zev[1])
            ins = sy.dma_start(
                out=R["zth"].rearrange("(hb p) t -> p hb t", p=128)
                [:, :, tb * TB3:(tb + 1) * TB3],
                in_=zt_stage.rearrange("p (hb t) -> p hb t", hb=KH))
            self.h_zst = C["d_st0"].inc(ins, 16)

    def p3b(self, tb_lo, tb_hi):
        R, C = self.R, self.C
        t, v, s, sy, g = self.t, self.v, self.s, self.sy, self.g
        self.dma_gate("d_st0")
        wi = R["wi"]
        W = R["POOLW"]
        w_dt = W[:, 0:KH * DI]
        ins = g.dma_start(out=w_dt.rearrange("p (kb h) -> p kb h", kb=KH),
                          in_=wi["W_dt"].rearrange("(kb p) h -> p kb h",
                                                   p=128))
        C["d_w0"].inc(ins, 16)
        t.wait_ge(C["d_w0"].h, C["d_w0"].v)
        M = R["POOLM"]
        zt = [M[:, 0:KH * TB3], M[:, KH * TB3:2 * KH * TB3]]
        o = 2 * KH * TB3
        dts = M[:, o:o + 2 * TB3]; o += 2 * TB3
        dtzs = M[:, o:o + 2 * TB3]; o += 2 * TB3
        scr = M[:, o:o + 4096].bitcast(F32); o += 4096
        bcs = M[:, o:o + 1024]; o += 1024
        zpar = [C["d_a0"], C["d_a1"]]
        zused = [0, 0]
        for tb in range(tb_lo, tb_hi):
            par = tb % 2
            sy.wait_ge(C["t"].h, zused[par])
            sy.wait_ge(C["v"].h, self.h_dtv[par])
            ins = sy.dma_start(
                out=zt[par].rearrange("p (hb t) -> p hb t", hb=KH),
                in_=R["zth"].rearrange("(hb p) t -> p hb t", p=128)
                [:, :, tb * TB3:(tb + 1) * TB3])
            wv = zpar[par].inc(ins, 16)
            t.wait_ge(zpar[par].h, wv)
            v.wait_ge(zpar[par].h, wv)
            for hb in range(KH):
                pp = hb % 2
                psd = R["pa"][0][:, pp * 512:(pp + 1) * 512]
                t.wait_ge(C["s"].h, self.h_dA[pp])
                ins = None
                for kb in range(KH):
                    ins = t.matmul(psd,
                                   w_dt[:, kb * DI + hb * 128:
                                        kb * DI + (hb + 1) * 128],
                                   zt[par][:, kb * TB3:(kb + 1) * TB3],
                                   start=(kb == 0), stop=(kb == KH - 1))
                C["t"].inc(ins)
                s.wait_ge(C["t"].h, C["t"].v)
                s.wait_ge(C["v"].h, self.h_dtv[pp])
                bcol = R["BDT"][:, hb:hb + 1]
                sa = scr[:, pp * 1024:pp * 1024 + 512]
                sbx = scr[:, pp * 1024 + 512:(pp + 1) * 1024]
                s.activation(out=sa, in_=psd, func=AF.Abs, bias=bcol)
                s.activation(out=sa, in_=sa, func=AF.Exp, scale=-1.0)
                s.activation(out=sa, in_=sa, func=AF.Ln, bias=1.0)
                ins = s.activation(out=sbx, in_=psd, func=AF.Relu, bias=bcol)
                self.h_dA[pp] = C["s"].inc(ins)
                v.wait_ge(C["s"].h, self.h_dA[pp])
                v.wait_ge(C["d_st0"].h, self.h_dtst[pp])
                dtt = dts[:, pp * TB3:(pp + 1) * TB3]
                dtzt = dtzs[:, pp * TB3:(pp + 1) * TB3]
                v.tensor_add(sa, sa, sbx)
                v.tensor_scalar(out=dtt, in0=sa, scalar1=0.001, scalar2=1.0,
                                op0=ALU.max, op1=ALU.min)
                ins = v.tensor_mul(dtzt, dtt,
                                   zt[par][:, hb * TB3:(hb + 1) * TB3])
                self.h_dtv[pp] = C["v"].inc(ins)
                sy.wait_ge(C["v"].h, self.h_dtv[pp])
                ins = sy.dma_start(
                    out=R["dth"][hb * 128:(hb + 1) * 128,
                                 tb * TB3:(tb + 1) * TB3], in_=dtt)
                C["d_st0"].inc(ins, 16)
                ins = sy.dma_start(
                    out=R["dtzh"][hb * 128:(hb + 1) * 128,
                                  tb * TB3:(tb + 1) * TB3], in_=dtzt)
                self.h_dtst[pp] = C["d_st0"].inc(ins, 16)
            zused[par] = C["t"].v
            t.wait_ge(C["s"].h, self.h_bcev)
            ins = None
            for kb in range(KH):
                ins = t.matmul(R["pa"][2][0:16, 0:512],
                               R["WBC"][:, kb * NS:(kb + 1) * NS],
                               zt[par][:, kb * TB3:(kb + 1) * TB3],
                               start=(kb == 0), stop=(kb == KH - 1))
            C["t"].inc(ins)
            for kb in range(KH):
                ins = t.matmul(R["pa"][2][0:16, 512:1024],
                               R["WBC"][:, KH * NS + kb * NS:
                                        KH * NS + (kb + 1) * NS],
                               zt[par][:, kb * TB3:(kb + 1) * TB3],
                               start=(kb == 0), stop=(kb == KH - 1))
            C["t"].inc(ins)
            zused[par] = C["t"].v
            s.wait_ge(C["t"].h, C["t"].v)
            s.wait_ge(C["d_st1"].h, self.h_gst[0])
            s.copy(out=bcs[0:16, 0:512], in_=R["pa"][2][0:16, 0:512])
            ins = s.copy(out=bcs[0:16, 512:1024],
                         in_=R["pa"][2][0:16, 512:1024])
            self.h_bcev = C["s"].inc(ins)
            sy.wait_ge(C["s"].h, self.h_bcev)
            ins = sy.dma_start(out=R["bch"][:, 0, tb * TB3:(tb + 1) * TB3],
                               in_=bcs[0:16, 0:512])
            C["d_st1"].inc(ins, 16)
            ins = sy.dma_start(out=R["bch"][:, 1, tb * TB3:(tb + 1) * TB3],
                               in_=bcs[0:16, 512:1024])
            self.h_gst[0] = C["d_st1"].inc(ins, 16)

    def p4a(self, ts_lo, ts_hi):
        R, C = self.R, self.C
        t, v, s, sy, g = self.t, self.v, self.s, self.sy, self.g
        self.dma_gate("d_st0", "d_st1")
        M = R["POOLM"]
        o = 0
        dtw = [None, None]
        dtzw = [None, None]
        for pq in range(2):
            dtw[pq] = M[:, o:o + KH * TSC]; o += KH * TSC
            dtzw[pq] = M[:, o:o + KH * TSC]; o += KH * TSC
        yac = M[:, o:o + KH * TSC]; o += KH * TSC
        bbc = M[:, o:o + 4 * TSC]; o += 4 * TSC
        scrA = M[:, o:o + 8 * TSC].bitcast(F32); o += 8 * TSC
        hhb = M[:, o:o + KH * TSC]; o += KH * TSC
        scrB = M[:, o:o + 4 * TSC]; o += 4 * TSC
        ldp = [C["d_a0"], C["d_a1"]]
        bcp = [C["d_bc0"], C["d_bc1"]]
        st = C["d_st0"]
        for ts in range(ts_lo, ts_hi):
            par = ts % 2
            sy.wait_ge(C["s"].h, self.h_dtuse_s[par])
            sy.wait_ge(C["v"].h, self.h_dtuse_v[par])
            ins = sy.dma_start(
                out=dtw[par].rearrange("p (hb t) -> p hb t", hb=KH),
                in_=R["dth"].rearrange("(hb p) t -> p hb t", p=128)
                [:, :, ts * TSC:(ts + 1) * TSC])
            ldp[par].inc(ins, 16)
            ins = sy.dma_start(
                out=dtzw[par].rearrange("p (hb t) -> p hb t", hb=KH),
                in_=R["dtzh"].rearrange("(hb p) t -> p hb t", p=128)
                [:, :, ts * TSC:(ts + 1) * TSC])
            wv_ld = ldp[par].inc(ins, 16)
            s.wait_ge(ldp[par].h, wv_ld)
            v.wait_ge(ldp[par].h, wv_ld)
            v.wait_ge(st.h, self.h_ygst)
            for si in range(NS):
                bp = si % 2
                g.wait_ge(C["v"].h, self.h_sdone[bp])
                ins = g.dma_start(
                    out=bbc[:, bp * 512:(bp + 1) * 512]
                    .rearrange("p (bc t) -> p bc t", bc=2),
                    in_=R["bch"][si:si + 1, :, ts * TSC:(ts + 1) * TSC]
                    .partition_broadcast(128).squeeze(1))
                wv_bc = bcp[bp].inc(ins, 16)
                v.wait_ge(bcp[bp].h, wv_bc)
                Bbc = bbc[:, bp * 512:bp * 512 + 256]
                Cbc = bbc[:, bp * 512 + 256:(bp + 1) * 512]
                for half in range(2):
                  hbs = range(half * 8, half * 8 + 8)
                  for hb in hbs:
                    ip = hb % 4
                    dA = scrA[:, ip * 256:(ip + 1) * 256]
                    uu = scrB[:, (hb % 2) * 256:(hb % 2) * 256 + 256]
                    hh = hhb[:, hb * TSC:(hb + 1) * TSC]
                    s.wait_ge(C["v"].h, self.h_scan[ip])
                    acol = R["ASC"][:, hb * NS + si:hb * NS + si + 1]
                    ins = s.activation(
                        out=dA, in_=dtw[par][:, hb * TSC:(hb + 1) * TSC],
                        func=AF.Exp, scale=acol)
                    self.h_dA[ip] = C["s"].inc(ins)
                    v.wait_ge(C["s"].h, self.h_dA[ip])
                    hcol = R["HST"][:, hb * NS + si:hb * NS + si + 1]
                    v.tensor_mul(uu, dtzw[par][:, hb * TSC:(hb + 1) * TSC],
                                 Bbc)
                    ins = v.tensor_tensor_scan(out=hh, data0=dA, data1=uu,
                                               initial=hcol, op0=ALU.mult,
                                               op1=ALU.add)
                    self.h_scan[ip] = C["v"].inc(ins)
                  v.wait_ge(C["v"].h, C["v"].v)
                  for hb in hbs:
                    hcol = R["HST"][:, hb * NS + si:hb * NS + si + 1]
                    v.tensor_copy(out=hcol,
                                  in_=hhb[:, (hb + 1) * TSC - 1:
                                          (hb + 1) * TSC])
                  for hb in hbs:
                    hh = hhb[:, hb * TSC:(hb + 1) * TSC]
                    hc = scrB[:, 512 + (hb % 2) * 256:512 + (hb % 2) * 256
                              + 256]
                    if si == 0:
                        ins = v.tensor_mul(
                            yac[:, hb * TSC:(hb + 1) * TSC], hh, Cbc)
                    else:
                        ins = v.tensor_mul(hc, hh, Cbc)
                        ins = v.tensor_tensor(
                            out=yac[:, hb * TSC:(hb + 1) * TSC],
                            in0=yac[:, hb * TSC:(hb + 1) * TSC],
                            in1=hc, op=ALU.add)
                  C["v"].inc(ins)
                  self.h_sdone[bp] = C["v"].v
                  yield
            self.h_dtuse_s[par] = C["s"].v
            self.h_dtuse_v[par] = C["v"].v
            sy.wait_ge(C["v"].h, C["v"].v)
            ins = sy.dma_start(
                out=R["zth"].rearrange("(hb p) t -> p hb t", p=128)
                [:, :, ts * TSC:(ts + 1) * TSC],
                in_=yac.rearrange("p (hb t) -> p hb t", hb=KH))
            self.h_ygst = st.inc(ins, 16)

    def p4b(self, dst, ts_lo, ts_hi):
        R, C = self.R, self.C
        t, v, s, sy, g = self.t, self.v, self.s, self.sy, self.g
        self.dma_gate("d_st0", "d_st1", "d_cp")
        W = R["POOLW"]
        w_out = W[:, 0:KH * D]
        ins = g.dma_start(out=w_out.rearrange("p (kb d) -> p kb d", kb=KH),
                          in_=R["wi"]["W_out"]
                          .rearrange("(kb p) d -> p kb d", p=128))
        C["d_w1"].inc(ins, 16)
        t.wait_ge(C["d_w1"].h, C["d_w1"].v)
        M = R["POOLM"]
        ya = [M[:, 0:KH * TSC], M[:, KH * TSC:2 * KH * TSC]]
        gt = [M[:, 2 * KH * TSC:3 * KH * TSC], M[:, 3 * KH * TSC:4 * KH * TSC]]
        ldp = [C["d_b0"], C["d_b1"]]
        for ts in range(ts_lo, ts_hi):
            par = ts % 2
            sy.wait_ge(C["t"].h, self.h_p4bmm[par])
            sy.wait_ge(C["v"].h, self.h_p4bv[par])
            ins = sy.dma_start(
                out=ya[par].rearrange("p (hb t) -> p hb t", hb=KH),
                in_=R["zth"].rearrange("(hb p) t -> p hb t", p=128)
                [:, :, ts * TSC:(ts + 1) * TSC])
            ldp[par].inc(ins, 16)
            ins = sy.dma_start(
                out=gt[par].rearrange("p (hb t) -> p hb t", hb=KH),
                in_=R["gateh"].rearrange("(hb p) t -> p hb t", p=128)
                [:, :, ts * TSC:(ts + 1) * TSC])
            wv = ldp[par].inc(ins, 16)
            v.wait_ge(ldp[par].h, wv)
            ins = None
            for hb in range(KH):
                ins = v.tensor_mul(ya[par][:, hb * TSC:(hb + 1) * TSC],
                                   ya[par][:, hb * TSC:(hb + 1) * TSC],
                                   gt[par][:, hb * TSC:(hb + 1) * TSC])
            self.h_p4bv[par] = C["v"].inc(ins)
            t.wait_ge(C["v"].h, self.h_p4bv[par])
            for tq in range(TSC // 128):
                q = (ts * TSC) // 128 + tq
                app = q % 2
                ps = R["pa"][2 + app]
                t.wait_ge(C["s"].h, self.h_ev[app])
                ins = None
                for dh in range(2):
                    for hb in range(KH):
                        ins = t.matmul(
                            ps[:, dh * 512:(dh + 1) * 512],
                            ya[par][:, hb * TSC + tq * 128:
                                    hb * TSC + tq * 128 + 128],
                            w_out[:, hb * D + dh * 512:
                                  hb * D + dh * 512 + 512],
                            start=(hb == 0), stop=(hb == KH - 1))
                self.h_p4bmm[par] = C["t"].inc(ins)
                s.wait_ge(C["t"].h, C["t"].v)
                s.wait_ge(C["d_acc"].h, self.h_acc[app])
                ev = R["EV"][:, app * D:(app + 1) * D]
                ins = s.copy(out=ev, in_=ps[:])
                self.h_ev[app] = C["s"].inc(ins)
                g.wait_ge(C["s"].h, self.h_ev[app])
                ins = g.dma_start(out=dst[q * 128:(q + 1) * 128, :], in_=ev,
                                  accum_op=ALU.add)
                self.h_acc[app] = C["d_acc"].inc(ins, 16)
                self.h_accq[q] = self.h_acc[app]
            yield


def kernel(L0_attn_x, L0_attn_norm_w, L0_attn_norm_b, L0_attn_W_in,
           L0_attn_W_gate, L0_attn_W_dt, L0_attn_b_dt, L0_attn_W_B,
           L0_attn_W_C, L0_attn_A_log, L0_attn_W_out, L0_ffn_shared_gate,
           L0_ffn_shared_up, L0_ffn_shared_down, L0_ffn_routed_gate,
           L0_ffn_routed_up, L0_ffn_routed_down, L0_ffn_Wr,
           L0_ffn_routing_bias, _stop=None):
    nc = build_nc(stop=_stop)
    xs = np.ascontiguousarray(np.asarray(L0_attn_x, dtype=np.float32))
    common = dict(
        norm_w=np.ascontiguousarray(L0_attn_norm_w, dtype=np.float32),
        norm_b=np.ascontiguousarray(L0_attn_norm_b, dtype=np.float32),
        W_in=np.ascontiguousarray(L0_attn_W_in, dtype=np.float32),
        W_gate=np.ascontiguousarray(L0_attn_W_gate, dtype=np.float32),
        W_dt=np.ascontiguousarray(L0_attn_W_dt, dtype=np.float32),
        b_dt=np.ascontiguousarray(L0_attn_b_dt, dtype=np.float32),
        W_B=np.ascontiguousarray(L0_attn_W_B, dtype=np.float32),
        W_C=np.ascontiguousarray(L0_attn_W_C, dtype=np.float32),
        A_log=np.ascontiguousarray(L0_attn_A_log, dtype=np.float32),
        W_out=np.ascontiguousarray(L0_attn_W_out, dtype=np.float32),
        sh_gate=np.ascontiguousarray(L0_ffn_shared_gate, dtype=np.float32),
        sh_up=np.ascontiguousarray(L0_ffn_shared_up, dtype=np.float32),
        sh_down=np.ascontiguousarray(L0_ffn_shared_down, dtype=np.float32),
        r_gate=np.ascontiguousarray(L0_ffn_routed_gate, dtype=np.float32),
        r_up=np.ascontiguousarray(L0_ffn_routed_up, dtype=np.float32),
        r_down=np.ascontiguousarray(L0_ffn_routed_down, dtype=np.float32),
        Wr=np.ascontiguousarray(L0_ffn_Wr, dtype=np.float32),
        r_bias=np.ascontiguousarray(L0_ffn_routing_bias, dtype=np.float32),
    )
    in_maps = [dict(common, x0=np.ascontiguousarray(xs[c])) for c in range(8)]
    res = run_bass_kernel_spmd(nc, in_maps, core_ids=list(range(8)))
    return np.stack([r["out"] for r in res.results], axis=0)
